# version 05 (final)
# baseline (speedup 1.0000x reference)
import math
from contextlib import ExitStack

import numpy as np
import ml_dtypes

import concourse.bass as bass
import concourse.mybir as mybir
from concourse.bass_utils import run_bass_kernel_spmd
from concourse.alu_op_type import AluOpType as AO

BF = ml_dtypes.bfloat16
F32 = mybir.dt.float32
F16 = mybir.dt.float16
BF16 = mybir.dt.bfloat16
I32 = mybir.dt.int32
U8 = mybir.dt.uint8
AF = mybir.ActivationFunctionType

NC = 8
D = 2048
CH = 16
TL = 1024
TH = 512
NEGBIG = -1e30
ATTN_MULT = 1.0 / math.sqrt(128.0)
DH = 128


class NS:
    pass


def NS2(**kw):
    n = NS()
    n.__dict__.update(kw)
    return n


ATT_LVL = 0
FUSED_DEBUG = False



def zig_blocks(c):
    return (c, 15 - c)


def rank_of_block(g):
    return g if g < 8 else 15 - g


def slot_of_block(g):
    return 0 if g < 8 else 1


def shard_x(x):
    out = []
    for c in range(NC):
        a, b = zig_blocks(c)
        tok = np.concatenate([x[a * TH:(a + 1) * TH], x[b * TH:(b + 1) * TH]], axis=0)
        out.append(np.ascontiguousarray(tok.T))
    return out


def unshard_x(shards):
    x = np.empty((8192, D), dtype=np.asarray(shards[0]).dtype)
    for c in range(NC):
        a, b = zig_blocks(c)
        t = np.asarray(shards[c]).T
        x[a * TH:(a + 1) * TH] = t[:TH]
        x[b * TH:(b + 1) * TH] = t[TH:]
    return x


def retile_io(w):
    i, o = w.shape
    return np.ascontiguousarray(
        w.reshape(i // 128, 128, o // 128, 128).transpose(2, 0, 1, 3))


def bias_cols(b):
    return np.ascontiguousarray(b.reshape(-1, 128).T)


def rope_tables(c):
    a, b = zig_blocks(c)
    pos = np.concatenate([np.arange(a * TH, (a + 1) * TH),
                          np.arange(b * TH, (b + 1) * TH)]).astype(np.float32)
    inv = (1.0 / (10000.0 ** (np.arange(0, DH, 2, dtype=np.float32) / DH)))
    emb = pos[:, None].astype(np.float32) * inv[None, :].astype(np.float32)
    emb = np.concatenate([emb, emb], axis=1)
    cos = np.cos(emb.astype(np.float64)).astype(np.float32).T
    sin = np.sin(emb.astype(np.float64)).astype(np.float32).T
    s = np.float64(DH) ** -0.5
    return (np.ascontiguousarray(cos), np.ascontiguousarray(sin),
            np.ascontiguousarray((cos.astype(np.float64) * s).astype(np.float32)),
            np.ascontiguousarray((sin.astype(np.float64) * s).astype(np.float32)))


def mask_tables():
    k = np.arange(128)[:, None]
    qp = np.arange(896)[None, :] - 384
    t1 = np.where(qp >= k, 0.0, NEGBIG).astype(np.float32)
    t2 = np.where(qp <= k, 0.0, NEGBIG).astype(np.float32)
    return t1, t2



def make_env(nc, st_):
    E = st_.enter_context
    env = NS()
    env.nc = nc
    env.sem_st = st_
    env.res = E(nc.sbuf_tensor("res", [128, CH * TL], F32))
    env.actbuf = E(nc.sbuf_tensor("actbuf", [128, CH * TL], BF16))
    env.ones_f = E(nc.sbuf_tensor("ones_f", [128, 1], F32))
    env.ones_b = E(nc.sbuf_tensor("ones_b", [128, 1], BF16))
    env.idn = E(nc.sbuf_tensor("idn", [128, 128], BF16))
    env.idn_d = nc.dram_tensor("c_idn", [128, 128], BF16, kind="ExternalInput")
    env.cpool = E(nc.sbuf_tensor("cpool", [128, 4], F32))
    env.s_env = E(nc.semaphore("s_env"))
    env.s_cp = E(nc.semaphore("s_cp"))
    return env


CP_EPS5 = 0
CP_EPS6 = 1
CP_TINY = 2
CP_ONE = 3


def env_consts_vector(env, vector):
    vector.memset(env.cpool[:, CP_EPS5:CP_EPS5 + 1], 1e-5)
    vector.memset(env.cpool[:, CP_EPS6:CP_EPS6 + 1], 1e-6)
    vector.memset(env.cpool[:, CP_TINY:CP_TINY + 1], 1e-20)
    vector.memset(env.cpool[:, CP_ONE:CP_ONE + 1], 1.0).then_inc(env.s_cp, 1)


def one_row(env, n=128):
    return env.ones_f[0:1, 0:1].broadcast_to([1, n])



def ln_make(nc, sem_st, buf_st, tag, prev=None):
    c = NS()
    c.tag = tag
    if prev is None:
        S = sem_st.enter_context
        c.s_sq = S(nc.semaphore(tag + "_sq"))
        c.s_mm = S(nc.semaphore(tag + "_mm"))
        c.s_row = S(nc.semaphore(tag + "_row"))
        c.s_dv = S(nc.semaphore(tag + "_dv"))
        c.b_sq = c.b_mm = c.b_row = c.b_dv = 0
    else:
        c.s_sq, c.s_mm, c.s_row, c.s_dv = (prev.s_sq, prev.s_mm, prev.s_row,
                                           prev.s_dv)
        c.b_sq = prev.b_sq + 16
        c.b_mm = prev.b_mm + 19
        c.b_row = prev.b_row + 1
        c.b_dv = prev.b_dv + 21
    c.dv_done = c.b_dv + 21
    E = buf_st.enter_context
    c.sq = E(nc.sbuf_tensor(tag + "_sqscr", [128, 2 * TL], BF16))
    c.rows = E(nc.sbuf_tensor(tag + "_rows", [1, 4 * TL], F32))
    c.mbc = E(nc.sbuf_tensor(tag + "_mbc", [128, TL], F32))
    c.rbc = E(nc.sbuf_tensor(tag + "_rbc", [128, TL], F32))
    c.scr = E(nc.sbuf_tensor(tag + "_scr", [128, TL], F32))
    c.ps_sx = E(nc.psum_tensor(tag + "_psx", [1, TL], F32))
    c.ps_sq = E(nc.psum_tensor(tag + "_psq", [1, TL], F32))
    c.ps_bc = E(nc.psum_tensor(tag + "_pbc", [128, TL], F32))
    return c


def ln_scalar(c, scalar, env, eps_col, src=None, src_off=0):
    src = env.res if src is None else src
    for ch in range(CH):
        if ch >= 2:
            scalar.wait_ge(c.s_mm, c.b_mm + ch)
        par = ch % 2
        scalar.activation(c.sq[:, par * TL:(par + 1) * TL],
                          src[:, src_off + ch * TL: src_off + (ch + 1) * TL],
                          AF.Square).then_inc(c.s_sq, 1)
    scalar.wait_ge(c.s_dv, c.b_dv + 3)
    scalar.wait_ge(env.s_cp, 1)
    scalar.activation(c.rows[0:1, 3 * TL:4 * TL], c.rows[0:1, 2 * TL:3 * TL],
                      AF.Ln, bias=env.cpool[0:1, eps_col:eps_col + 1], scale=1.0)
    scalar.activation(c.rows[0:1, TL:2 * TL], c.rows[0:1, 3 * TL:4 * TL],
                      AF.Exp, scale=-0.5).then_inc(c.s_row, 1)


def ln_tensor(c, tensor, env, src=None, src_off=0, wait=None):
    src = env.res if src is None else src
    if wait is not None:
        tensor.wait_ge(*wait)
    for h in range(2):
        for ch in range(CH):
            ins = tensor.matmul(c.ps_sx[0:1, h * TH:(h + 1) * TH],
                                env.ones_f[:, 0:1],
                                src[:, src_off + ch * TL + h * TH:
                                    src_off + ch * TL + h * TH + TH],
                                start=(ch == 0), stop=(ch == CH - 1))
    ins.then_inc(c.s_mm, 1)
    for ch in range(CH):
        tensor.wait_ge(c.s_sq, c.b_sq + ch + 1)
        par = ch % 2
        for h in range(2):
            ins = tensor.matmul(c.ps_sq[0:1, h * TH:(h + 1) * TH],
                                env.ones_b[:, 0:1],
                                c.sq[:, par * TL + h * TH: par * TL + h * TH + TH],
                                start=(ch == 0), stop=(ch == CH - 1))
        ins.then_inc(c.s_mm, 1)
    tensor.wait_ge(c.s_dv, c.b_dv + 1)
    for h in range(2):
        ins = tensor.matmul(c.ps_bc[:, h * TH:(h + 1) * TH], one_row(env),
                            c.rows[0:1, h * TH:(h + 1) * TH], start=True, stop=True)
    ins.then_inc(c.s_mm, 1)
    tensor.wait_ge(c.s_dv, c.b_dv + 4)
    tensor.wait_ge(c.s_row, c.b_row + 1)
    for h in range(2):
        ins = tensor.matmul(c.ps_bc[:, h * TH:(h + 1) * TH], one_row(env),
                            c.rows[0:1, TL + h * TH: TL + h * TH + TH],
                            start=True, stop=True)
    ins.then_inc(c.s_mm, 1)


def ln_vector(c, vector, env, out, out_off=0, src=None, src_off=0):
    src = env.res if src is None else src
    inv_d = 1.0 / D
    vector.wait_ge(c.s_mm, c.b_mm + 17)
    vector.tensor_scalar(c.rows[0:1, 0:TL], c.ps_sx[0:1, :], inv_d, None,
                         AO.mult).then_inc(c.s_dv, 1)
    vector.tensor_scalar(c.rows[0:1, 2 * TL:3 * TL], c.ps_sq[0:1, :], inv_d, None,
                         AO.mult).then_inc(c.s_dv, 1)
    vector.tensor_tensor(c.rows[0:1, 3 * TL:4 * TL], c.rows[0:1, 0:TL],
                         c.rows[0:1, 0:TL], AO.mult)
    vector.tensor_tensor(c.rows[0:1, 2 * TL:3 * TL], c.rows[0:1, 2 * TL:3 * TL],
                         c.rows[0:1, 3 * TL:4 * TL], AO.subtract).then_inc(c.s_dv, 1)
    vector.wait_ge(c.s_mm, c.b_mm + 18)
    vector.tensor_copy(c.mbc[:, :], c.ps_bc[:, :]).then_inc(c.s_dv, 1)
    vector.wait_ge(c.s_mm, c.b_mm + 19)
    vector.tensor_copy(c.rbc[:, :], c.ps_bc[:, :]).then_inc(c.s_dv, 1)
    for ch in range(CH):
        vector.tensor_tensor(c.scr[:, :],
                             src[:, src_off + ch * TL: src_off + (ch + 1) * TL],
                             c.mbc[:, :], AO.subtract)
        vector.tensor_tensor(out[:, out_off + ch * TL: out_off + (ch + 1) * TL],
                             c.scr[:, :], c.rbc[:, :],
                             AO.mult).then_inc(c.s_dv, 1)


LN_DV_DONE = 21



def proj_make(nc, sem_st, buf_st, tag, no):
    c = NS()
    c.tag, c.no = tag, no
    S = sem_st.enter_context
    c.s_wl = S(nc.semaphore(tag + "_wl"))
    c.s_mm = S(nc.semaphore(tag + "_mm"))
    c.s_ev = S(nc.semaphore(tag + "_ev"))
    E = buf_st.enter_context
    c.wbuf = E(nc.sbuf_tensor(tag + "_wbuf", [128, 2 * CH * 128], BF16))
    c.ps = None
    return c


def proj_psum(nc, c, ps_st):
    E = ps_st.enter_context
    c.ps = [E(nc.psum_tensor(c.tag + "_ps%d" % p, [128, TL], F32))
            for p in range(2)]


def proj_loads(c, q, wdram, first_wait=None):
    for j in range(c.no):
        if j == 0 and first_wait is not None:
            q.wait_ge(*first_wait)
        if j >= 2:
            q.wait_ge(c.s_mm, j - 1)
        q.dma_start(out=c.wbuf[:, (j % 2) * CH * 128:((j % 2) + 1) * CH * 128]
                    .rearrange("p (i f) -> p i f", i=CH),
                    in_=wdram[j, :, :, :].rearrange("i p f -> p i f")
                    ).then_inc(c.s_wl, 16)


def proj_chunk_tensor(c, tensor, j, rhs, rhs_off):
    tensor.wait_ge(c.s_wl, 16 * (j + 1))
    if j >= 2:
        tensor.wait_ge(c.s_ev, j - 1)
    w = c.wbuf[:, (j % 2) * CH * 128:((j % 2) + 1) * CH * 128]
    ps = c.ps[j % 2]
    for h in range(2):
        for i in range(CH):
            ins = tensor.matmul(
                ps[:, h * TH:(h + 1) * TH],
                w[:, i * 128:(i + 1) * 128],
                rhs[:, rhs_off + i * TL + h * TH: rhs_off + i * TL + h * TH + TH],
                start=(i == 0), stop=(i == CH - 1))
    ins.then_inc(c.s_mm, 1)


def run_block(nc, **emitters):
    with nc.Block() as b:
        for name in ("sync", "gpsimd", "vector", "tensor", "scalar"):
            if name in emitters:
                getattr(b, name)(emitters[name])



def build_l0qkv():
    nc = bass.Bass("TRN2", target_bir_lowering=False, detect_race_conditions=False)
    x_d = nc.dram_tensor("x", [D, TL], F32, kind="ExternalInput")
    w_d = nc.dram_tensor("wqkv", [48, CH, 128, 128], F32, kind="ExternalInput")
    b_d = nc.dram_tensor("bqkv", [128, 48], F32, kind="ExternalInput")
    q_o = nc.dram_tensor("q", [D, TL], BF16, kind="ExternalOutput")
    kst_o = nc.dram_tensor("kstage", [D + CH, TL], BF16, kind="ExternalOutput")
    vst_o = nc.dram_tensor("vstage", [CH * TL, 128], BF16, kind="ExternalOutput")

    with ExitStack() as st_:
        E = st_.enter_context
        env = make_env(nc, st_)
        bq = E(nc.sbuf_tensor("bq", [128, 48], F32))
        q_sb = E(nc.sbuf_tensor("q_sb", [128, CH * TL], BF16))
        s_x = E(nc.semaphore("s_x"))
        s_bq = E(nc.semaphore("s_bq"))
        pj = proj_make(nc, st_, st_, "qkv", 48)
        ln_st = st_.enter_context(ExitStack())
        ln = ln_make(nc, st_, ln_st, "ln1")

        def a_sync(sync):
            sync.dma_start(out=env.idn[:, :], in_=env.idn_d[:, :]
                           ).then_inc(env.s_env, 16)
            sync.dma_start(out=env.res[:, :].rearrange("p (c t) -> p c t", c=CH),
                           in_=x_d[:, :].rearrange("(c p) t -> p c t", p=128)
                           ).then_inc(s_x, 16)
            sync.dma_start(out=bq[:, :], in_=b_d[:, :]).then_inc(env.s_env, 16)

        def a_vector(vector):
            env_consts_vector(env, vector)
            vector.memset(env.ones_f[:, :], 1.0).then_inc(env.s_env, 1)
            vector.memset(env.ones_b[:, :], 1.0).then_inc(env.s_env, 1)
            ln_vector(ln, vector, env, env.actbuf, 0)

        def a_tensor(tensor):
            ln_tensor(ln, tensor, env, wait=(s_x, 16))

        def a_scalar(scalar):
            scalar.wait_ge(s_x, 16)
            ln_scalar(ln, scalar, env, CP_EPS5)

        run_block(nc, sync=a_sync, vector=a_vector, tensor=a_tensor,
                  scalar=a_scalar)
        ln_st.close()

        emit_l0qkv_body(nc, st_, env, ln, pj, q_sb, bq, w_d, q_o, kst_o, vst_o)
    return nc


def emit_l0qkv_body(nc, st_, env, ln, pj, q_sb, bq, w_d, q_o, kst_o, vst_o,
                    env_ready=32, join=True):
    with ExitStack() as ps_:
        E = ps_.enter_context
        kv_sb = E(nc.sbuf_tensor("kv_sb", [128, 2 * TL], BF16))
        ksq = E(nc.sbuf_tensor("ksq", [128, 2 * TL], BF16))
        knorm = E(nc.sbuf_tensor("knorm", [1, 2 * TL], BF16))
        vt_sb = E(nc.sbuf_tensor("vt_sb", [128, 2 * TL], BF16))
        s_ksq = env.sem_st.enter_context(nc.semaphore("s_ksq"))
        s_knp = env.sem_st.enter_context(nc.semaphore("s_knp"))
        s_knv = env.sem_st.enter_context(nc.semaphore("s_knv"))
        s_trp = env.sem_st.enter_context(nc.semaphore("s_trp"))
        s_trv = env.sem_st.enter_context(nc.semaphore("s_trv"))
        s_kst = env.sem_st.enter_context(nc.semaphore("s_kst"))
        s_vst = env.sem_st.enter_context(nc.semaphore("s_vst"))
        s_st = env.sem_st.enter_context(nc.semaphore("s_st"))
        proj_psum(nc, pj, ps_)
        ps_kn = E(nc.psum_tensor("ps_kn", [1, TL], F32))
        ps_tr = [E(nc.psum_tensor("ps_tr%d" % p, [128, 128], BF16))
                 for p in range(2)]

        def b_sync(sync):
            if join:
                sync.wait_ge(s_kst, 16 * 16)
                sync.wait_ge(s_vst, 16 * 16)
                sync.wait_ge(s_st, 16 * (17 if q_o is not None else 16))

        def b_gpsimd(gpsimd):
            proj_loads(pj, gpsimd, w_d)

        def b_vector(vector):
            vector.wait_ge(env.s_env, env_ready)
            for j in range(48):
                vector.wait_ge(pj.s_mm, j + 1)
                ps = pj.ps[j % 2]
                if j < 16:
                    vector.tensor_scalar(q_sb[:, j * TL:(j + 1) * TL], ps[:, :],
                                         bq[:, j:j + 1], ATTN_MULT,
                                         AO.add, AO.mult).then_inc(pj.s_ev, 1)
                elif j < 32:
                    h = j - 16
                    if h >= 2:
                        vector.wait_ge(s_kst, 16 * (h - 1))
                    vector.tensor_scalar(kv_sb[:, (j % 2) * TL:(j % 2 + 1) * TL],
                                         ps[:, :], bq[:, j:j + 1], None,
                                         AO.add).then_inc(pj.s_ev, 1)
                    vector.wait_ge(s_knp, h + 1)
                    if h >= 2:
                        vector.wait_ge(s_st, 16 * (h - 1))
                    vector.tensor_copy(knorm[0:1, (h % 2) * TL:(h % 2 + 1) * TL],
                                       ps_kn[0:1, :]).then_inc(s_knv, 1)
                else:
                    h = j - 32
                    if h == 0:
                        vector.wait_ge(s_kst, 16 * 15)
                    elif h == 1:
                        vector.wait_ge(s_kst, 16 * 16)
                    else:
                        vector.wait_ge(s_trp, 8 * (h - 1))
                    vector.tensor_scalar(kv_sb[:, (j % 2) * TL:(j % 2 + 1) * TL],
                                         ps[:, :], bq[:, j:j + 1], None,
                                         AO.add).then_inc(pj.s_ev, 1)
                    if h >= 2:
                        vector.wait_ge(s_vst, 16 * (h - 1))
                    for qt in range(8):
                        m = h * 8 + qt
                        vector.wait_ge(s_trp, m + 1)
                        vector.tensor_copy(
                            vt_sb[:, (h % 2) * TL + qt * 128:
                                  (h % 2) * TL + (qt + 1) * 128],
                            ps_tr[m % 2][:, :]).then_inc(s_trv, 1)

        def b_tensor(tensor):
            tensor.wait_ge(ln.s_dv, ln.dv_done)
            for j in range(48):
                proj_chunk_tensor(pj, tensor, j, env.actbuf, 0)
                if 16 <= j < 32:
                    h = j - 16
                    tensor.wait_ge(s_ksq, h + 1)
                    if h >= 1:
                        tensor.wait_ge(s_knv, h)
                    tensor.matmul(ps_kn[0:1, 0:TH], env.ones_b[:, 0:1],
                                  ksq[:, (h % 2) * TL:(h % 2) * TL + TH],
                                  start=True, stop=True)
                    tensor.matmul(ps_kn[0:1, TH:TL], env.ones_b[:, 0:1],
                                  ksq[:, (h % 2) * TL + TH:(h % 2 + 1) * TL],
                                  start=True, stop=True).then_inc(s_knp, 1)
                elif j >= 32:
                    h = j - 32
                    tensor.wait_ge(pj.s_ev, j + 1)
                    for qt in range(8):
                        m = h * 8 + qt
                        if m >= 2:
                            tensor.wait_ge(s_trv, m - 1)
                        tensor.transpose(
                            ps_tr[m % 2][:, :],
                            kv_sb[:, (j % 2) * TL + qt * 128:
                                  (j % 2) * TL + (qt + 1) * 128],
                            env.idn[:, :]).then_inc(s_trp, 1)

        def b_scalar(scalar):
            for j in range(16, 48):
                scalar.wait_ge(pj.s_ev, j + 1)
                if j < 32:
                    h = j - 16
                    if h >= 2:
                        scalar.wait_ge(s_knp, h - 1)
                    scalar.activation(ksq[:, (h % 2) * TL:(h % 2 + 1) * TL],
                                      kv_sb[:, (j % 2) * TL:(j % 2 + 1) * TL],
                                      AF.Square).then_inc(s_ksq, 1)
                    scalar.dma_start(
                        out=kst_o[h * 128:(h + 1) * 128, :],
                        in_=kv_sb[:, (j % 2) * TL:(j % 2 + 1) * TL]
                    ).then_inc(s_kst, 16)
                    scalar.wait_ge(s_knv, h + 1)
                    scalar.dma_start(
                        out=kst_o[D + h:D + h + 1, :],
                        in_=knorm[0:1, (h % 2) * TL:(h % 2 + 1) * TL]
                    ).then_inc(s_st, 16)
                else:
                    h = j - 32
                    scalar.wait_ge(s_trv, 8 * (h + 1))
                    scalar.dma_start(
                        out=vst_o[h * TL:(h + 1) * TL, :]
                        .rearrange("(q p) d -> p q d", p=128),
                        in_=vt_sb[:, (h % 2) * TL:(h % 2 + 1) * TL]
                        .rearrange("p (q d) -> p q d", q=8)).then_inc(s_vst, 16)
            if q_o is not None:
                scalar.wait_ge(pj.s_ev, 16)
                scalar.dma_start(out=q_o[:, :]
                                 .rearrange("(c p) t -> p c t", p=128),
                                 in_=q_sb[:, :]
                                 .rearrange("p (c t) -> p c t", c=CH)
                                 ).then_inc(s_st, 16)

        run_block(nc, sync=b_sync, gpsimd=b_gpsimd, vector=b_vector,
                  tensor=b_tensor, scalar=b_scalar)
        return NS2(s_kst=s_kst, s_vst=s_vst, s_st=s_st)



class AttnCfg:
    def __init__(self, l1):
        self.l1 = l1
        self.heads = 16
        if l1:
            self.kv_of = lambda h: h // 4
            self.n_norm = 4
            self.kst_rows = 516
            self.vst_rows = 4 * TL
        else:
            self.kv_of = lambda h: h
            self.n_norm = 16
            self.kst_rows = D + CH
            self.vst_rows = CH * TL

    def pieces(self, c, qc):
        b = zig_blocks(c)[qc]
        if self.l1:
            out = []
            if b >= 1:
                out.append((b - 1, "T2"))
            out.append((b, "T1"))
            return out
        return [(kb, "T1" if kb == b else None) for kb in range(b + 1)]


def emit_attn(nc, st_, env, cfg, q_sb, ctx_sb, kg_d, vg_d, t1, t2, s_done,
              s_dve_out=None, sync_wait=(), scalar_wait=None, tag="at",
              dbg=None, gate=()):
    with ExitStack() as ps_:
        E = ps_.enter_context
        NH = cfg.heads
        NR = NC * cfg.n_norm
        sqscr = E(nc.sbuf_tensor(tag + "_sq", [128, 2 * TL], BF16))
        rows = E(nc.sbuf_tensor(tag + "_rows", [1, 2 * TL + 256], F32))
        knb = E(nc.sbuf_tensor(tag + "_knb", [128, TL], BF16))
        kmcol = E(nc.sbuf_tensor(tag + "_kmc", [128, 1], F32))
        krow_sb = E(nc.sbuf_tensor(tag + "_krow", [1, 128], F32))
        kscr_d = nc.dram_tensor(tag + "_kscr",
                                [128, 1], F32, kind="Internal")
        negones = E(nc.sbuf_tensor(tag + "_neg1", [1, 1], F32))
        bcols = E(nc.sbuf_tensor(tag + "_bc", [128, 2 * NH], F32))
        kbuf = E(nc.sbuf_tensor(tag + "_kbuf", [128, 2 * TH], BF16))
        vbuf = E(nc.sbuf_tensor(tag + "_vbuf", [128, 2 * TH], BF16))
        pt = E(nc.sbuf_tensor(tag + "_pt", [128, 2 * TL], BF16))
        rbc = E(nc.sbuf_tensor(tag + "_rbc", [128, TH], F32))
        s_sy = env.sem_st.enter_context(nc.semaphore(tag + "_sy"))
        s_v = env.sem_st.enter_context(nc.semaphore(tag + "_v"))
        s_t = env.sem_st.enter_context(nc.semaphore(tag + "_t"))
        s_s = env.sem_st.enter_context(nc.semaphore(tag + "_s"))
        s_kv = env.sem_st.enter_context(nc.semaphore(tag + "_kv"))
        s_qkh = env.sem_st.enter_context(nc.semaphore(tag + "_qkh"))
        s_rdy = env.sem_st.enter_context(nc.semaphore(tag + "_rdy"))
        s_exp = env.sem_st.enter_context(nc.semaphore(tag + "_exp"))
        s_pv = env.sem_st.enter_context(nc.semaphore(tag + "_pv"))
        s_dve = (s_dve_out if s_dve_out is not None
                 else env.sem_st.enter_context(nc.semaphore(tag + "_dve")))
        s_bc = env.sem_st.enter_context(nc.semaphore(tag + "_bcm"))
        ps_st = [E(nc.psum_tensor(tag + "_pst%d" % p, [128, TL], F32))
                 for p in range(2)]
        ps_acc = E(nc.psum_tensor(tag + "_acc", [128, TH], F32))
        ps_row = E(nc.psum_tensor(tag + "_prow", [1, TL], F32))
        ps_bc = E(nc.psum_tensor(tag + "_pbc", [128, TH], F32))
        ps_sum = ps_row[0:1, 0:TH]

        if gate:
            def g_all(eng):
                for w_ in gate:
                    eng.wait_ge(*w_)
            with nc.Block() as gblk:
                gblk.sync(g_all)
                gblk.gpsimd(g_all)
                gblk.vector(g_all)
                gblk.tensor(g_all)
                gblk.scalar(g_all)

        def p_sync(sync):
            for w_ in sync_wait:
                sync.wait_ge(*w_)
            for r in range(NC):
                sync.dma_start(
                    out=knb[r * cfg.n_norm:(r + 1) * cfg.n_norm, :],
                    in_=kg_d[(r + 1) * cfg.kst_rows - cfg.n_norm:
                             (r + 1) * cfg.kst_rows, :]).then_inc(s_sy, 16)
            sync.wait_ge(s_v, 1)
            sync.dma_start(out=kscr_d[0:NR, :], in_=kmcol[0:NR, 0:1]
                           ).then_inc(s_sy, 16)
            sync.wait_ge(s_sy, 16 * NC + 16)
            sync.dma_start(out=krow_sb[0:1, 0:NR],
                           in_=kscr_d[0:NR, :].rearrange("a b -> b a")
                           ).then_inc(s_sy, 16)

        def p_vector(vector):
            vector.memset(negones[:, :], -1.0)
            vector.wait_ge(s_sy, 16 * NC)
            vector.reduce_max(kmcol[0:NR, 0:1], knb[0:NR, :],
                              axis=mybir.AxisListType.X).then_inc(s_v, 1)
            vector.wait_ge(s_sy, 16 * NC + 32)
            vector.reduce_max(rows[0:1, 2 * TL:2 * TL + 1],
                              krow_sb[0:1, 0:NR],
                              axis=mybir.AxisListType.X).then_inc(s_v, 1)
            for h in range(NH):
                vector.wait_ge(s_s, 2 * h + 3)
                vector.reduce_max(rows[0:1, 2 * TL + 2 + 2 * h:
                                        2 * TL + 3 + 2 * h],
                                  rows[0:1, 0:TH], axis=mybir.AxisListType.X)
                vector.reduce_max(rows[0:1, 2 * TL + 3 + 2 * h:
                                        2 * TL + 4 + 2 * h],
                                  rows[0:1, TH:TL], axis=mybir.AxisListType.X)
                vector.tensor_scalar(
                    rows[0:1, 2 * TL + 2 + 2 * h: 2 * TL + 4 + 2 * h],
                    rows[0:1, 2 * TL + 2 + 2 * h: 2 * TL + 4 + 2 * h],
                    rows[0:1, 2 * TL + 1: 2 * TL + 2], 1.0,
                    AO.mult, AO.add).then_inc(s_v, 1)
                vector.wait_ge(s_t, 2 * h + 2)
                vector.tensor_copy(bcols[:, 2 * h:2 * h + 2],
                                   ps_bc[:, 0:2]).then_inc(s_v, 1)

        def p_scalar(scalar):
            if scalar_wait is not None:
                scalar.wait_ge(*scalar_wait)
            scalar.wait_ge(env.s_cp, 1)
            scalar.wait_ge(s_v, 2)
            scalar.activation(rows[0:1, 2 * TL + 1:2 * TL + 2],
                              rows[0:1, 2 * TL:2 * TL + 1], AF.Ln,
                              bias=env.cpool[0:1, CP_TINY:CP_TINY + 1])
            scalar.activation(rows[0:1, 2 * TL + 1:2 * TL + 2],
                              rows[0:1, 2 * TL + 1:2 * TL + 2], AF.Exp,
                              scale=0.5).then_inc(s_s, 1)
            for h in range(NH):
                if h >= 2:
                    scalar.wait_ge(s_t, 2 * h - 3)
                scalar.activation(sqscr[:, (h % 2) * TL:(h % 2 + 1) * TL],
                                  q_sb[:, h * TL:(h + 1) * TL],
                                  AF.Square).then_inc(s_s, 1)
                scalar.wait_ge(s_t, 2 * h + 1)
                if h >= 1:
                    scalar.wait_ge(s_v, 2 * h + 1)
                scalar.activation(rows[0:1, TL:2 * TL], ps_row[0:1, :], AF.Ln,
                                  bias=env.cpool[0:1, CP_TINY:CP_TINY + 1])
                scalar.activation(rows[0:1, 0:TL], rows[0:1, TL:2 * TL],
                                  AF.Exp, scale=0.5).then_inc(s_s, 1)

        def p_tensor(tensor):
            for h in range(NH):
                tensor.wait_ge(s_s, 2 * h + 2)
                for hh in range(2):
                    ins = tensor.matmul(ps_row[0:1, hh * TH:(hh + 1) * TH],
                                        env.ones_b[:, 0:1],
                                        sqscr[:, (h % 2) * TL + hh * TH:
                                              (h % 2) * TL + hh * TH + TH],
                                        start=True, stop=True)
                ins.then_inc(s_t, 1)
                tensor.wait_ge(s_v, 2 * h + 3)
                tensor.matmul(ps_bc[:, 0:2],
                              negones[0:1, 0:1].broadcast_to([1, 128]),
                              rows[0:1, 2 * TL + 2 + 2 * h:2 * TL + 4 + 2 * h],
                              start=True, stop=True).then_inc(s_t, 1)

        run_block(nc, sync=p_sync, vector=p_vector, tensor=p_tensor,
                  scalar=p_scalar)

        if dbg is not None:
            def dbg_sync(sync):
                sync.wait_ge(s_v, 2 * NH + 2)
                sync.dma_start(out=dbg[0][:, :], in_=bcols[:, :]
                               ).then_inc(s_v, 16)
                sync.dma_start(out=dbg[1][:, :], in_=rows[0:1, 2 * TL:
                                                         2 * TL + 256]
                               ).then_inc(s_v, 16)
            run_block(nc, sync=dbg_sync)

        engs = [nc.sync, nc.gpsimd, nc.vector, nc.tensor, nc.scalar]
        idx = [e.partition_id() for e in engs]
        for ci in nc.Switch(engines=engs, index=idx, n=NC):
            emit_attn_body(nc, env, cfg, ci, q_sb, ctx_sb, kg_d, vg_d, t1, t2,
                           kbuf, vbuf, pt, rbc, bcols, rows,
                           ps_st, ps_acc, ps_sum, ps_bc,
                           s_kv, s_qkh, s_rdy, s_exp, s_pv, None, s_dve,
                           s_bc, s_done)


def emit_attn_body(nc, env, cfg, ci, q_sb, ctx_sb, kg_d, vg_d, t1, t2,
                   kbuf, vbuf, pt, rbc, bcols, rows,
                   ps_st, ps_acc, ps_sum, ps_bc,
                   s_kv, s_qkh, s_rdy, s_exp, s_pv, s_pcd, s_dve, s_bc,
                   s_done):
    sync, vector, tensor, scalar = nc.sync, nc.vector, nc.tensor, nc.scalar
    sched = []
    for h in range(cfg.heads):
        for qc in range(2):
            ps = cfg.pieces(ci, qc)
            for i, (kb, mk) in enumerate(ps):
                sched.append((h, qc, kb, mk, i == 0, i == len(ps) - 1))

    for pc, (h, qc, kb, mk, first, last) in enumerate(sched):
        if pc >= 2:
            sync.wait_ge(s_pv, 2 * (pc - 1))
        if ATT_LVL >= 4:
            sync.sem_inc(s_kv, 32)
            continue
        r, s = rank_of_block(kb), slot_of_block(kb)
        kvh = cfg.kv_of(h)
        krow = r * cfg.kst_rows + kvh * 128
        sync.dma_start(out=kbuf[:, (pc % 2) * TH:(pc % 2 + 1) * TH],
                       in_=kg_d[krow:krow + 128, s * TH:(s + 1) * TH]
                       ).then_inc(s_kv, 16)
        vrow = r * cfg.vst_rows + kvh * TL + s * TH
        sync.dma_start(out=vbuf[:, (pc % 2) * TH:(pc % 2 + 1) * TH]
                       .rearrange("p (t d) -> p t d", t=4),
                       in_=vg_d[vrow:vrow + TH, :]
                       .rearrange("(t p) d -> p t d", p=128)
                       ).then_inc(s_kv, 16)

    nch = 0
    for pc, (h, qc, kb, mk, first, last) in enumerate(sched):
        tensor.wait_ge(s_kv, 32 * (pc + 1))
        if first and nch > 0:
            tensor.wait_ge(s_dve, 3 * nch)
        for half in range(2):
            m = 2 * pc + half
            if m >= 2:
                tensor.wait_ge(s_exp, m - 1)
            if ATT_LVL >= 3:
                tensor.sem_inc(s_qkh, 1)
                continue
            for kt2 in range(2):
                kt = half * 2 + kt2
                ins = tensor.matmul(
                    ps_st[m % 2][:, kt2 * TH:(kt2 + 1) * TH],
                    kbuf[:, (pc % 2) * TH + kt * 128:
                         (pc % 2) * TH + (kt + 1) * 128],
                    q_sb[:, h * TL + qc * TH: h * TL + (qc + 1) * TH],
                    start=True, stop=True)
            ins.then_inc(s_qkh, 1)
        for half in range(2):
            m = 2 * pc + half
            tensor.wait_ge(s_exp, m + 1)
            if 1 <= ATT_LVL <= 4:
                tensor.sem_inc(s_pv, 1)
                continue
            for kt2 in range(2):
                kt = half * 2 + kt2
                tensor.matmul(ps_acc[:, :],
                              vbuf[:, (pc % 2) * TH + kt * 128:
                                   (pc % 2) * TH + (kt + 1) * 128],
                              pt[:, (m % 2) * TL + kt2 * TH:
                                 (m % 2) * TL + (kt2 + 1) * TH],
                              start=(first and kt == 0),
                              stop=(last and kt == 3))
                ins = tensor.matmul(ps_sum, env.ones_b[:, 0:1],
                                    pt[:, (m % 2) * TL + kt2 * TH:
                                       (m % 2) * TL + (kt2 + 1) * TH],
                                    start=(first and kt == 0),
                                    stop=(last and kt == 3))
            ins.then_inc(s_pv, 1)
        if last:
            nch += 1
            if ATT_LVL >= 1 or ATT_LVL in (-6, -8, -9, -10):
                tensor.sem_inc(s_bc, 1)
            else:
                tensor.wait_ge(s_dve, 3 * nch - 2)
                tensor.matmul(ps_bc[:, :], one_row(env),
                              rows[0:1, TL:TL + TH], start=True,
                              stop=True).then_inc(s_bc, 1)
    tensor.sem_inc(s_done, 1)

    nch = 0
    for pc, (h, qc, kb, mk, first, last) in enumerate(sched):
        for half in range(2):
            m = 2 * pc + half
            vector.wait_ge(s_qkh, m + 1)
            if mk and ATT_LVL < 3:
                tab = t1 if mk == "T1" else t2
                for kt2 in range(2):
                    kt = half * 2 + kt2
                    ins = vector.tensor_tensor(
                        ps_st[m % 2][:, kt2 * TH:(kt2 + 1) * TH],
                        ps_st[m % 2][:, kt2 * TH:(kt2 + 1) * TH],
                        tab[:, 384 - 128 * kt: 896 - 128 * kt], AO.add)
                ins.then_inc(s_rdy, 1)
            else:
                vector.sem_inc(s_rdy, 1)
        if last:
            nch += 1
            vector.wait_ge(s_pv, 2 * (pc + 1))
            if ATT_LVL >= 1:
                vector.sem_inc(s_dve, 3)
            elif ATT_LVL == -7:
                vector.tensor_copy(rows[0:1, TL + TH:TL + 2 * TH], ps_sum)
                vector.reciprocal(rows[0:1, TL:TL + TH],
                                  rows[0:1, TL + TH:TL + 2 * TH]
                                  ).then_inc(s_dve, 1)
                vector.wait_ge(s_bc, nch)
                vector.tensor_copy(rbc[:, :], ps_bc[:, :]).then_inc(s_dve, 1)
                vector.sem_inc(s_dve, 1)
            elif ATT_LVL == -8:
                vector.tensor_copy(rows[0:1, TL + TH:TL + 2 * TH], ps_sum)
                vector.reciprocal(rows[0:1, TL:TL + TH],
                                  rows[0:1, TL + TH:TL + 2 * TH]
                                  ).then_inc(s_dve, 1)
                vector.sem_inc(s_dve, 2)
            elif ATT_LVL == -10:
                vector.tensor_copy(rows[0:1, TL + TH:TL + 2 * TH], ps_sum
                                   ).then_inc(s_dve, 1)
                vector.sem_inc(s_dve, 2)
            elif ATT_LVL == -12:
                vector.sem_inc(s_dve, 3)
            elif ATT_LVL == -13:
                vector.reciprocal(rows[0:1, TL:TL + TH],
                                  rows[0:1, TL + TH:TL + 2 * TH]
                                  ).then_inc(s_dve, 1)
                vector.sem_inc(s_dve, 2)
            elif ATT_LVL == -14:
                vector.tensor_copy(rows[0:1, TL + TH:TL + 2 * TH], ps_sum
                                   ).then_inc(s_dve, 1)
                vector.sem_inc(s_dve, 2)
            elif ATT_LVL == -9:
                vector.sem_inc(s_dve, 3)
            else:
                vector.tensor_copy(rows[0:1, TL + TH:TL + 2 * TH], ps_sum)
                vector.reciprocal(rows[0:1, TL:TL + TH],
                                  rows[0:1, TL + TH:TL + 2 * TH]
                                  ).then_inc(s_dve, 1)
                vector.wait_ge(s_bc, nch)
                vector.tensor_copy(rbc[:, :], ps_bc[:, :]).then_inc(s_dve, 1)
                vector.tensor_tensor(
                    ctx_sb[:, h * TL + qc * TH: h * TL + (qc + 1) * TH],
                    ps_acc[:, :], rbc[:, :], AO.mult).then_inc(s_dve, 1)
    vector.sem_inc(s_done, 1)

    for pc, (h, qc, kb, mk, first, last) in enumerate(sched):
        for half in range(2):
            m = 2 * pc + half
            scalar.wait_ge(s_rdy, m + 1)
            if m >= 2:
                scalar.wait_ge(s_pv, m - 1)
            if ATT_LVL >= 2:
                scalar.sem_inc(s_exp, 1)
            else:
                scalar.activation(pt[:, (m % 2) * TL:(m % 2 + 1) * TL],
                                  ps_st[m % 2][:, :], AF.Exp,
                                  bias=bcols[:, 2 * h + qc:2 * h + qc + 1]
                                  ).then_inc(s_exp, 1)
    scalar.sem_inc(s_done, 1)
    sync.sem_inc(s_done, 1)
    nc.gpsimd.sem_inc(s_done, 1)


def build_l0attn():
    nc = bass.Bass("TRN2", target_bir_lowering=False, detect_race_conditions=False)
    cfg = AttnCfg(l1=False)
    q_d = nc.dram_tensor("q", [D, TL], BF16, kind="ExternalInput")
    kg_d = nc.dram_tensor("kg", [NC * cfg.kst_rows, TL], BF16,
                          kind="ExternalInput")
    vg_d = nc.dram_tensor("vg", [NC * cfg.vst_rows, 128], BF16,
                          kind="ExternalInput")
    t1_d = nc.dram_tensor("c_t1", [128, 896], F32, kind="ExternalInput")
    ctx_o = nc.dram_tensor("ctx", [D, TL], BF16, kind="ExternalOutput")

    with ExitStack() as st_:
        E = st_.enter_context
        env = make_env(nc, st_)
        q_sb = E(nc.sbuf_tensor("q_sb", [128, CH * TL], BF16))
        t1 = E(nc.sbuf_tensor("t1", [128, 896], F32))
        s_q = E(nc.semaphore("s_q"))
        s_done = E(nc.semaphore("s_done"))
        s_att_dve = E(nc.semaphore("s_att_dve"))
        s_out = E(nc.semaphore("s_out"))

        def a_sync(sync):
            sync.dma_start(out=env.idn[:, :], in_=env.idn_d[:, :]
                           ).then_inc(env.s_env, 16)
            sync.dma_start(out=q_sb[:, :].rearrange("p (c t) -> p c t", c=CH),
                           in_=q_d[:, :].rearrange("(c p) t -> p c t", p=128)
                           ).then_inc(s_q, 16)
            sync.dma_start(out=t1[:, :], in_=t1_d[:, :]).then_inc(s_q, 16)

        def a_vector(vector):
            env_consts_vector(env, vector)
            vector.memset(env.ones_f[:, :], 1.0).then_inc(env.s_env, 1)
            vector.memset(env.ones_b[:, :], 1.0).then_inc(env.s_env, 1)
            vector.wait_ge(env.s_env, 18)
            vector.wait_ge(s_q, 32)
            vector.sem_inc(env.s_env, 1)

        run_block(nc, sync=a_sync, vector=a_vector)

        def gate(eng):
            eng.wait_ge(env.s_env, 19)

        with nc.Block() as blk:
            blk.tensor(gate)
            blk.scalar(gate)
            blk.gpsimd(gate)
            blk.sync(gate)

        emit_attn(nc, st_, env, cfg, q_sb, env.actbuf, kg_d, vg_d, t1, None,
                  s_done, s_dve_out=s_att_dve)

        def z_sync(sync):
            sync.wait_ge(s_done, 5)
            sync.wait_ge(s_att_dve, 96)
            sync.dma_start(out=ctx_o[:, :].rearrange("(c p) t -> p c t", p=128),
                           in_=env.actbuf[:, :].rearrange("p (c t) -> p c t",
                                                          c=CH)
                           ).then_inc(s_out, 16)
            sync.wait_ge(s_out, 16)

        run_block(nc, sync=z_sync)
    return nc



def build_proj_res_ln(with_bias, tag):
    nc = bass.Bass("TRN2", target_bir_lowering=False,
                   detect_race_conditions=False)
    x_d = nc.dram_tensor("x", [D, TL], F32, kind="ExternalInput")
    ctx_d = nc.dram_tensor("ctx", [D, TL], BF16, kind="ExternalInput")
    w_d = nc.dram_tensor("wproj", [CH, CH, 128, 128], F32, kind="ExternalInput")
    if with_bias:
        b_d = nc.dram_tensor("bproj", [128, CH], F32, kind="ExternalInput")
    xo_d = nc.dram_tensor("xo", [D, TL], F32, kind="ExternalOutput")
    xb_d = nc.dram_tensor("xb", [D, TL], BF16, kind="ExternalOutput")

    with ExitStack() as st_:
        E = st_.enter_context
        env = make_env(nc, st_)
        bp = E(nc.sbuf_tensor("bp", [128, CH], F32))
        s_x = E(nc.semaphore("s_x"))
        s_st = E(nc.semaphore("s_st"))
        pj = proj_make(nc, st_, st_, "pj" + tag, CH)

        def a_sync(sync):
            sync.dma_start(out=env.res[:, :].rearrange("p (c t) -> p c t", c=CH),
                           in_=x_d[:, :].rearrange("(c p) t -> p c t", p=128)
                           ).then_inc(s_x, 16)
            sync.dma_start(out=env.actbuf[:, :].rearrange("p (c t) -> p c t",
                                                          c=CH),
                           in_=ctx_d[:, :].rearrange("(c p) t -> p c t", p=128)
                           ).then_inc(s_x, 16)
            if with_bias:
                sync.dma_start(out=bp[:, :], in_=b_d[:, :]).then_inc(s_x, 16)

        def a_vector(vector):
            env_consts_vector(env, vector)
            vector.memset(env.ones_f[:, :], 1.0)
            vector.memset(env.ones_b[:, :], 1.0).then_inc(env.s_env, 1)

        run_block(nc, sync=a_sync, vector=a_vector)

        with ExitStack() as ps_:
            proj_psum(nc, pj, ps_)

            def b_gpsimd(gpsimd):
                proj_loads(pj, gpsimd, w_d)

            def b_tensor(tensor):
                tensor.wait_ge(s_x, 32)
                for j in range(CH):
                    proj_chunk_tensor(pj, tensor, j, env.actbuf, 0)

            def b_vector(vector):
                vector.wait_ge(s_x, 16 * (3 if with_bias else 2))
                for j in range(CH):
                    vector.wait_ge(pj.s_mm, j + 1)
                    ps = pj.ps[j % 2]
                    vector.scalar_tensor_tensor(
                        env.res[:, j * TL:(j + 1) * TL], ps[:, :],
                        bp[:, j:j + 1] if with_bias else 0.0,
                        env.res[:, j * TL:(j + 1) * TL],
                        AO.add, AO.add).then_inc(pj.s_ev, 1)

            run_block(nc, gpsimd=b_gpsimd, tensor=b_tensor, vector=b_vector)

        with ExitStack() as ln_st:
            ln = ln_make(nc, st_, ln_st, "ln" + tag)

            def c_sync(sync):
                sync.wait_ge(ln.s_dv, ln.dv_done)
                sync.dma_start(out=xo_d[:, :].rearrange("(c p) t -> p c t", p=128),
                               in_=env.res[:, :].rearrange("p (c t) -> p c t",
                                                           c=CH)
                               ).then_inc(s_st, 16)
                sync.dma_start(out=xb_d[:, :].rearrange("(c p) t -> p c t", p=128),
                               in_=env.actbuf[:, :].rearrange("p (c t) -> p c t",
                                                              c=CH)
                               ).then_inc(s_st, 16)
                sync.wait_ge(s_st, 32)

            def c_tensor(tensor):
                ln_tensor(ln, tensor, env, wait=(pj.s_ev, CH))

            def c_scalar(scalar):
                scalar.wait_ge(pj.s_ev, CH)
                ln_scalar(ln, scalar, env, CP_EPS5)

            def c_vector(vector):
                ln_vector(ln, vector, env, env.actbuf, 0)

            run_block(nc, sync=c_sync, tensor=c_tensor, scalar=c_scalar,
                      vector=c_vector)
    return nc



def dequant_scales_host(scales, o0, no, ntile):
    s = scales[:, o0:o0 + no]
    g = s.shape[0]
    return np.ascontiguousarray(
        s.T.reshape(ntile, 128, g))


def build_dequant():
    nc = bass.Bass("TRN2", target_bir_lowering=False,
                   detect_race_conditions=False)
    gp_d = nc.dram_tensor("gu_pk", [1408, 256], I32, kind="ExternalInput")
    gs_d = nc.dram_tensor("gu_sc", [11, 128, 16], F16, kind="ExternalInput")
    dp_d = nc.dram_tensor("dn_pk", [256, 704], I32, kind="ExternalInput")
    ds_d = nc.dram_tensor("dn_sc", [2, 128, 44], F16, kind="ExternalInput")
    gst_o = nc.dram_tensor("gustage", [176, 16384], BF16, kind="ExternalOutput")
    dst_o = nc.dram_tensor("dstage", [88, 16384], BF16, kind="ExternalOutput")

    with ExitStack() as st_:
        E = st_.enter_context
        env = make_env(nc, st_)
        s_ld = E(nc.semaphore("s_ld"))
        s_sc = E(nc.semaphore("s_sc"))

        def a_sync(sync):
            sync.dma_start(out=env.idn[:, :], in_=env.idn_d[:, :]
                           ).then_inc(env.s_env, 16)

        def a_vector(vector):
            env_consts_vector(env, vector)

        run_block(nc, sync=a_sync, vector=a_vector)
        emit_dequant(nc, st_, env, gp_d, gs_d, dp_d, ds_d, gst_o, dst_o,
                     s_ld, s_sc, None)
    return nc


def emit_dequant(nc, st_, env, gp_d, gs_d, dp_d, ds_d, gst_o, dst_o,
                 s_ld, s_sc, s_fin, sync_wait=(), gate=()):
    with ExitStack() as ps_:
        E = ps_.enter_context
        spk = E(nc.sbuf_tensor("dq_pk", [128, 11 * 256 + 8 * 176], I32))
        ssc = E(nc.sbuf_tensor("dq_sc", [128, 11 * 16 + 2 * 44], F32))
        scr = [E(nc.sbuf_tensor("dq_s%d" % i, [128, 1024], F32))
               for i in range(4)]
        hi8 = E(nc.sbuf_tensor("dq_hi8", [128, 1024], U8))
        lo8 = E(nc.sbuf_tensor("dq_lo8", [128, 1024], U8))
        wt = E(nc.sbuf_tensor("dq_wt", [128, 2 * 2048], BF16))
        wsb = [E(nc.sbuf_tensor("dq_wsb%d" % i, [128, 128], BF16))
               for i in range(2)]
        s_dq = env.sem_st.enter_context(nc.semaphore("dq_dq"))
        s_tp = env.sem_st.enter_context(nc.semaphore("dq_tp"))
        s_ev = env.sem_st.enter_context(nc.semaphore("dq_ev"))
        s_st = env.sem_st.enter_context(nc.semaphore("dq_st"))
        ps_tr = [E(nc.psum_tensor("dq_ptr%d" % p, [128, 128], BF16))
                 for p in range(2)]

        units = []
        for t in range(11):
            units.append(dict(pk=t * 256, w=256, ng=16, sc=t * 16,
                              stage=gst_o, rows=[t * 16 + i for i in range(16)]))
        for dt_ in range(2):
            for su in range(4):
                units.append(dict(pk=11 * 256 + dt_ * 704 + su * 176, w=176,
                                  ng=11, sc=11 * 16 + dt_ * 44 + su * 11,
                                  stage=dst_o,
                                  rows=[dt_ * 44 + su * 11 + i
                                        for i in range(11)]))

        if gate:
            def dg_all(eng):
                for w_ in gate:
                    eng.wait_ge(*w_)
            with nc.Block() as gblk:
                gblk.sync(dg_all)
                gblk.gpsimd(dg_all)
                gblk.vector(dg_all)
                gblk.tensor(dg_all)
                gblk.scalar(dg_all)

        def d_sync(sync):
            for w_ in sync_wait:
                sync.wait_ge(*w_)
            sync.dma_start(out=spk[:, 0:11 * 256]
                           .rearrange("p (t w) -> p t w", t=11),
                           in_=gp_d[:, :].rearrange("(t p) w -> p t w", p=128)
                           ).then_inc(s_ld, 16)
            sync.dma_start(out=spk[:, 11 * 256:]
                           .rearrange("p (t w) -> p t w", t=2),
                           in_=dp_d[:, :].rearrange("(t p) w -> p t w", p=128)
                           ).then_inc(s_ld, 16)

        def d_gpsimd(gpsimd):
            gpsimd.dma_start(out=ssc[:, 0:11 * 16]
                             .rearrange("p (t g) -> p t g", t=11),
                             in_=gs_d[:, :, :].rearrange("t p g -> p t g")
                             ).then_inc(s_sc, 16)
            gpsimd.dma_start(out=ssc[:, 11 * 16:]
                             .rearrange("p (t g) -> p t g", t=2),
                             in_=ds_d[:, :, :].rearrange("t p g -> p t g")
                             ).then_inc(s_sc, 16)

        def d_vector(vector):
            vector.wait_ge(s_ld, 32)
            vector.wait_ge(s_sc, 32)
            mtot = 0
            for ui, un in enumerate(units):
                W = un["w"]
                np_ = W * 4
                u8 = spk[:, un["pk"]:un["pk"] + W].bitcast(U8)
                if ui >= 2:
                    prev = sum(len(units[k]["rows"]) for k in range(ui - 1))
                    vector.wait_ge(s_tp, prev)
                wt_o = (ui % 2) * 2048
                vector.tensor_scalar(hi8[:, 0:np_], u8, 4, None,
                                     AO.logical_shift_right)
                vector.tensor_scalar(lo8[:, 0:np_], u8, 15, None,
                                     AO.bitwise_and)
                for par, src8 in ((1, hi8), (0, lo8)):
                    A, B, C, Dr = scr
                    vector.tensor_copy(A[:, 0:np_], src8[:, 0:np_])
                    vector.tensor_scalar(B[:, 0:np_], A[:, 0:np_], 8.0, -16.0,
                                         AO.is_ge, AO.mult)
                    vector.scalar_tensor_tensor(C[:, 0:np_], B[:, 0:np_], 0.5,
                                                A[:, 0:np_], AO.mult, AO.add)
                    vector.tensor_scalar(B[:, 0:np_], B[:, 0:np_], 0.125, 1.0,
                                         AO.mult, AO.add)
                    vector.tensor_scalar(A[:, 0:np_], C[:, 0:np_], -4.0, 0.0,
                                         AO.add, AO.max)
                    vector.tensor_scalar(Dr[:, 0:np_], A[:, 0:np_], -2.0, 0.0,
                                         AO.add, AO.max)
                    vector.scalar_tensor_tensor(A[:, 0:np_], A[:, 0:np_], 0.5,
                                                Dr[:, 0:np_], AO.mult, AO.add)
                    vector.scalar_tensor_tensor(C[:, 0:np_], C[:, 0:np_], 0.5,
                                                A[:, 0:np_], AO.mult, AO.add)
                    vector.tensor_tensor(A[:, 0:np_], C[:, 0:np_], B[:, 0:np_],
                                         AO.mult)
                    for g in range(un["ng"]):
                        dst = wt[:, wt_o + g * 128: wt_o + (g + 1) * 128]
                        dst = dst.rearrange("p (f two) -> p two f", two=2)
                        ins = vector.tensor_scalar(
                            dst[:, par, :], A[:, g * 64:(g + 1) * 64],
                            ssc[:, un["sc"] + g: un["sc"] + g + 1], None,
                            AO.mult)
                ins.then_inc(s_dq, 1)
                for i in range(len(un["rows"])):
                    m = mtot + i
                    vector.wait_ge(s_tp, m + 1)
                    if m >= 2:
                        vector.wait_ge(s_st, 16 * (m - 1))
                    vector.tensor_copy(wsb[m % 2][:, :], ps_tr[m % 2][:, :]
                                       ).then_inc(s_ev, 1)
                mtot += len(un["rows"])

        def d_tensor(tensor):
            tensor.wait_ge(env.s_env, 16)
            m = 0
            for ui, un in enumerate(units):
                tensor.wait_ge(s_dq, ui + 1)
                wt_o = (ui % 2) * 2048
                for i in range(len(un["rows"])):
                    if m >= 2:
                        tensor.wait_ge(s_ev, m - 1)
                    tensor.transpose(ps_tr[m % 2][:, :],
                                     wt[:, wt_o + i * 128: wt_o + (i + 1) * 128],
                                     env.idn[:, :]).then_inc(s_tp, 1)
                    m += 1

        def d_scalar(scalar):
            m = 0
            for ui, un in enumerate(units):
                for i, row in enumerate(un["rows"]):
                    scalar.wait_ge(s_ev, m + 1)
                    scalar.dma_start(
                        out=un["stage"][row:row + 1, :]
                        .rearrange("o (p f) -> (o p) f", p=128),
                        in_=wsb[m % 2][:, :]).then_inc(s_st, 16)
                    m += 1
            scalar.wait_ge(s_st, 16 * m)
            if s_fin is not None:
                scalar.sem_inc(s_fin, 1)

        run_block(nc, sync=d_sync, gpsimd=d_gpsimd, vector=d_vector,
                  tensor=d_tensor, scalar=d_scalar)



def build_l0ffn():
    nc = bass.Bass("TRN2", target_bir_lowering=False,
                   detect_race_conditions=False)
    x_d = nc.dram_tensor("x", [D, TL], F32, kind="ExternalInput")
    xb_d = nc.dram_tensor("xb", [D, TL], BF16, kind="ExternalInput")
    gug_d = nc.dram_tensor("gug", [NC * 176, 16384], BF16, kind="ExternalInput")
    dng_d = nc.dram_tensor("dng", [NC * 88, 16384], BF16, kind="ExternalInput")
    xo_d = nc.dram_tensor("xo", [D, TL], F32, kind="ExternalOutput")

    with ExitStack() as st_:
        E = st_.enter_context
        env = make_env(nc, st_)
        s_x = E(nc.semaphore("s_x"))
        s_st = E(nc.semaphore("s_st"))
        s_fv = E(nc.semaphore("s_fv"))

        def a_sync(sync):
            sync.dma_start(out=env.res[:, :].rearrange("p (c t) -> p c t", c=CH),
                           in_=x_d[:, :].rearrange("(c p) t -> p c t", p=128)
                           ).then_inc(s_x, 16)
            sync.dma_start(out=env.actbuf[:, :].rearrange("p (c t) -> p c t",
                                                          c=CH),
                           in_=xb_d[:, :].rearrange("(c p) t -> p c t", p=128)
                           ).then_inc(s_x, 16)

        def a_vector(vector):
            env_consts_vector(env, vector)
            vector.memset(env.ones_f[:, :], 1.0)
            vector.memset(env.ones_b[:, :], 1.0).then_inc(env.s_env, 1)

        run_block(nc, sync=a_sync, vector=a_vector)

        emit_l0ffn(nc, st_, env, gug_d, dng_d, s_x, s_fv)

        def z_sync(sync):
            sync.wait_ge(s_fv, 2)
            sync.dma_start(out=xo_d[:, :].rearrange("(c p) t -> p c t", p=128),
                           in_=env.res[:, :].rearrange("p (c t) -> p c t", c=CH)
                           ).then_inc(s_st, 16)
            sync.wait_ge(s_st, 16)

        run_block(nc, sync=z_sync)
    return nc


def emit_l0ffn(nc, st_, env, gug_d, dng_d, s_x, s_fv, x_wait=32,
               sync_wait=None, tensor_waits=(), dbg_y=None):
    with ExitStack() as ps_:
        E = ps_.enter_context
        h_g = E(nc.sbuf_tensor("f0_h", [128, 11 * TH], BF16))
        y = E(nc.sbuf_tensor("f0_y", [128, CH * TH], F32))
        gwb = E(nc.sbuf_tensor("f0_gw", [128, 2 * 2048], BF16))
        uwb = E(nc.sbuf_tensor("f0_uw", [128, 2 * 2048], BF16))
        dwb = E(nc.sbuf_tensor("f0_dw", [128, 2 * 1408], BF16))
        sgb = E(nc.sbuf_tensor("f0_sg", [128, TH], BF16))
        sqs = E(nc.sbuf_tensor("f0_sqs", [128, 2 * TH], BF16))
        rrow = E(nc.sbuf_tensor("f0_rr", [1, 2 * TH], F32))
        rbc = E(nc.sbuf_tensor("f0_rbc", [128, TH], F32))
        yt = E(nc.sbuf_tensor("f0_yt", [128, TH], F32))
        s_wl = env.sem_st.enter_context(nc.semaphore("f0_wl"))
        s_dl = env.sem_st.enter_context(nc.semaphore("f0_dl"))
        s_mm = env.sem_st.enter_context(nc.semaphore("f0_mm"))
        s_sg = env.sem_st.enter_context(nc.semaphore("f0_sg"))
        s_hv = env.sem_st.enter_context(nc.semaphore("f0_hv"))
        s_dm = env.sem_st.enter_context(nc.semaphore("f0_dm"))
        s_ye = env.sem_st.enter_context(nc.semaphore("f0_ye"))
        s_sq = env.sem_st.enter_context(nc.semaphore("f0_sq"))
        s_ssm = env.sem_st.enter_context(nc.semaphore("f0_ssm"))
        s_rw = env.sem_st.enter_context(nc.semaphore("f0_rw"))
        ps_g = E(nc.psum_tensor("f0_pg", [128, TH], F32))
        ps_u = E(nc.psum_tensor("f0_pu", [128, TH], F32))
        ps_d = [E(nc.psum_tensor("f0_pd%d" % p, [128, TH], F32))
                for p in range(2)]
        ps_ss = E(nc.psum_tensor("f0_pss", [1, TH], F32))
        ps_bc = E(nc.psum_tensor("f0_pbc", [128, TH], F32))

        NU = 11
        NG = 4

        def f_sync(sync):
            for w_ in (sync_wait or ()):
                sync.wait_ge(*w_)
            nw = 0
            nd = 0
            for tc in range(2):
                for g in range(NG):
                    for u in range(NU):
                        iw = nw
                        nw += 1
                        if iw >= 2:
                            sync.wait_ge(s_mm, 2 * iw - 2)
                        par = iw % 2
                        sync.dma_start(
                            out=gwb[:, par * 2048:(par + 1) * 2048]
                            .rearrange("p (i f) -> p i f", i=CH),
                            in_=gug_d[g * 176 + u * 16: g * 176 + (u + 1) * 16,
                                      :].rearrange("i (p f) -> p i f", p=128)
                            ).then_inc(s_wl, 16)
                        sync.dma_start(
                            out=uwb[:, par * 2048:(par + 1) * 2048]
                            .rearrange("p (i f) -> p i f", i=CH),
                            in_=gug_d[(g + 4) * 176 + u * 16:
                                      (g + 4) * 176 + (u + 1) * 16, :]
                            .rearrange("i (p f) -> p i f", p=128)
                            ).then_inc(s_wl, 16)
                    for oc in range(CH):
                        idn_ = nd
                        nd += 1
                        if idn_ >= 2:
                            sync.wait_ge(s_dm, idn_ - 1)
                        r, dt_ = oc // 2, oc % 2
                        sync.dma_start(
                            out=dwb[:, (idn_ % 2) * 1408:(idn_ % 2 + 1) * 1408]
                            .rearrange("p (i f) -> p i f", i=NU),
                            in_=dng_d[r * 88 + dt_ * 44 + NU * g:
                                      r * 88 + dt_ * 44 + NU * (g + 1), :]
                            .rearrange("i (p f) -> p i f", p=128)
                            ).then_inc(s_dl, 16)

        def f_tensor(tensor):
            tensor.wait_ge(s_x, x_wait)
            for w_ in tensor_waits:
                tensor.wait_ge(*w_)
            nw = 0
            nd = 0
            for tc in range(2):
                for g in range(NG):
                    for u in range(NU):
                        iw = nw
                        nw += 1
                        tensor.wait_ge(s_wl, 32 * (iw + 1))
                        if iw >= 1:
                            tensor.wait_ge(s_sg, iw)
                        par = iw % 2
                        for i in range(CH):
                            ins = tensor.matmul(
                                ps_g[:, :],
                                gwb[:, par * 2048 + i * 128:
                                    par * 2048 + (i + 1) * 128],
                                env.actbuf[:, i * TL + tc * TH:
                                           i * TL + tc * TH + TH],
                                start=(i == 0), stop=(i == CH - 1))
                        ins.then_inc(s_mm, 1)
                        if iw >= 1:
                            tensor.wait_ge(s_hv, iw)
                        for i in range(CH):
                            ins = tensor.matmul(
                                ps_u[:, :],
                                uwb[:, par * 2048 + i * 128:
                                    par * 2048 + (i + 1) * 128],
                                env.actbuf[:, i * TL + tc * TH:
                                           i * TL + tc * TH + TH],
                                start=(i == 0), stop=(i == CH - 1))
                        ins.then_inc(s_mm, 1)
                    for oc in range(CH):
                        idn_ = nd
                        nd += 1
                        tensor.wait_ge(s_dl, 16 * (idn_ + 1))
                        tensor.wait_ge(s_hv, nw)
                        if idn_ >= 2:
                            tensor.wait_ge(s_ye, idn_ - 1)
                        for i in range(NU):
                            ins = tensor.matmul(
                                ps_d[idn_ % 2][:, :],
                                dwb[:, (idn_ % 2) * 1408 + i * 128:
                                    (idn_ % 2) * 1408 + (i + 1) * 128],
                                h_g[:, i * TH:(i + 1) * TH],
                                start=(i == 0), stop=(i == NU - 1))
                        ins.then_inc(s_dm, 1)
                for oc in range(CH):
                    tensor.wait_ge(s_sq, tc * CH + oc + 1)
                    tensor.matmul(ps_ss[0:1, :], env.ones_b[:, 0:1],
                                  sqs[:, (oc % 2) * TH:(oc % 2 + 1) * TH],
                                  start=(oc == 0), stop=(oc == CH - 1)
                                  ).then_inc(s_ssm, 1)
                tensor.wait_ge(s_rw, tc + 1)
                tensor.matmul(ps_bc[:, :], one_row(env),
                              rrow[0:1, tc * TH:(tc + 1) * TH],
                              start=True, stop=True).then_inc(s_ssm, 1)

        def f_scalar(scalar):
            nw = 0
            for tc in range(2):
                for g in range(NG):
                    for u in range(NU):
                        iw = nw
                        nw += 1
                        scalar.wait_ge(s_mm, 2 * iw + 1)
                        if iw >= 1:
                            scalar.wait_ge(s_hv, iw)
                        scalar.activation(sgb[:, :], ps_g[:, :],
                                          AF.Silu).then_inc(s_sg, 1)
                for oc in range(CH):
                    scalar.wait_ge(s_ye, (tc * NG + NG - 1) * CH + oc + 1)
                    j = tc * CH + oc
                    if j >= 2:
                        scalar.wait_ge(s_ssm, 17 * ((j - 2) // CH) +
                                       ((j - 2) % CH) + 1)
                    scalar.activation(sqs[:, (oc % 2) * TH:(oc % 2 + 1) * TH],
                                      y[:, oc * TH:(oc + 1) * TH],
                                      AF.Square).then_inc(s_sq, 1)
                scalar.wait_ge(s_ssm, 17 * tc + 16)
                scalar.wait_ge(env.s_cp, 1)
                scalar.activation(rrow[0:1, tc * TH:(tc + 1) * TH],
                                  ps_ss[0:1, :], AF.Ln, scale=1.0 / D,
                                  bias=env.cpool[0:1, CP_EPS6:CP_EPS6 + 1])
                scalar.activation(rrow[0:1, tc * TH:(tc + 1) * TH],
                                  rrow[0:1, tc * TH:(tc + 1) * TH],
                                  AF.Exp, scale=-0.5).then_inc(s_rw, 1)

        def f_vector(vector):
            nw = 0
            nd = 0
            for tc in range(2):
                for g in range(NG):
                    for u in range(NU):
                        iw = nw
                        nw += 1
                        vector.wait_ge(s_sg, iw + 1)
                        vector.wait_ge(s_mm, 2 * iw + 2)
                        vector.tensor_tensor(h_g[:, u * TH:(u + 1) * TH],
                                             sgb[:, :], ps_u[:, :],
                                             AO.mult).then_inc(s_hv, 1)
                    for oc in range(CH):
                        idn_ = nd
                        nd += 1
                        vector.wait_ge(s_dm, idn_ + 1)
                        if g == 0:
                            vector.tensor_copy(y[:, oc * TH:(oc + 1) * TH],
                                               ps_d[idn_ % 2][:, :]
                                               ).then_inc(s_ye, 1)
                        else:
                            vector.tensor_tensor(y[:, oc * TH:(oc + 1) * TH],
                                                 y[:, oc * TH:(oc + 1) * TH],
                                                 ps_d[idn_ % 2][:, :],
                                                 AO.add).then_inc(s_ye, 1)
                vector.wait_ge(s_ssm, 17 * tc + 17)
                vector.tensor_copy(rbc[:, :], ps_bc[:, :])
                for oc in range(CH):
                    vector.tensor_tensor(yt[:, :], y[:, oc * TH:(oc + 1) * TH],
                                         rbc[:, :], AO.mult)
                    ins = vector.tensor_tensor(
                        env.res[:, oc * TL + tc * TH: oc * TL + tc * TH + TH],
                        env.res[:, oc * TL + tc * TH: oc * TL + tc * TH + TH],
                        yt[:, :], AO.add)
                ins.then_inc(s_fv, 1)

        run_block(nc, sync=f_sync, tensor=f_tensor, scalar=f_scalar,
                  vector=f_vector)



def build_l1qkv():
    nc = bass.Bass("TRN2", target_bir_lowering=False,
                   detect_race_conditions=False)
    x_d = nc.dram_tensor("x", [D, TL], F32, kind="ExternalInput")
    w_d = nc.dram_tensor("wqkv1", [24, CH, 128, 128], F32, kind="ExternalInput")
    qg_d = nc.dram_tensor("qn_w", [128, 1], F32, kind="ExternalInput")
    kg_d = nc.dram_tensor("kn_w", [128, 1], F32, kind="ExternalInput")
    cos_d = nc.dram_tensor("c_cos", [128, TL], F32, kind="ExternalInput")
    sin_d = nc.dram_tensor("c_sin", [128, TL], F32, kind="ExternalInput")
    cosq_d = nc.dram_tensor("c_cosq", [128, TL], F32, kind="ExternalInput")
    sinq_d = nc.dram_tensor("c_sinq", [128, TL], F32, kind="ExternalInput")
    q_o = nc.dram_tensor("q2", [D, TL], BF16, kind="ExternalOutput")
    kst_o = nc.dram_tensor("kstage1", [516, TL], BF16, kind="ExternalOutput")
    vst_o = nc.dram_tensor("vstage1", [4 * TL, 128], BF16, kind="ExternalOutput")

    with ExitStack() as st_:
        E = st_.enter_context
        env = make_env(nc, st_)
        q_sb = E(nc.sbuf_tensor("q2_sb", [128, CH * TL], BF16))
        cosT = E(nc.sbuf_tensor("cosT", [128, TL], F32))
        sinT = E(nc.sbuf_tensor("sinT", [128, TL], F32))
        cosqT = E(nc.sbuf_tensor("cosqT", [128, TL], F32))
        sinqT = E(nc.sbuf_tensor("sinqT", [128, TL], F32))
        gains = E(nc.sbuf_tensor("gains", [128, 2], F32))
        graw = E(nc.sbuf_tensor("graw", [128, 2], F32))
        s_x = E(nc.semaphore("s_x"))
        pj = proj_make(nc, st_, st_, "qkv1", 24)
        ln_st = st_.enter_context(ExitStack())
        ln = ln_make(nc, st_, ln_st, "ln3")

        def a_sync(sync):
            sync.dma_start(out=env.idn[:, :], in_=env.idn_d[:, :]
                           ).then_inc(env.s_env, 16)
            sync.dma_start(out=env.res[:, :].rearrange("p (c t) -> p c t", c=CH),
                           in_=x_d[:, :].rearrange("(c p) t -> p c t", p=128)
                           ).then_inc(s_x, 16)
            sync.dma_start(out=cosT[:, :], in_=cos_d[:, :]).then_inc(env.s_env, 16)
            sync.dma_start(out=sinT[:, :], in_=sin_d[:, :]).then_inc(env.s_env, 16)
            sync.dma_start(out=cosqT[:, :], in_=cosq_d[:, :]
                           ).then_inc(env.s_env, 16)
            sync.dma_start(out=sinqT[:, :], in_=sinq_d[:, :]
                           ).then_inc(env.s_env, 16)
            sync.dma_start(out=graw[:, 0:1], in_=qg_d[:, :]).then_inc(env.s_env, 16)
            sync.dma_start(out=graw[:, 1:2], in_=kg_d[:, :]).then_inc(env.s_env, 16)

        def a_vector(vector):
            env_consts_vector(env, vector)
            vector.memset(env.ones_f[:, :], 1.0).then_inc(env.s_env, 1)
            vector.memset(env.ones_b[:, :], 1.0).then_inc(env.s_env, 1)
            vector.wait_ge(env.s_env, 114)
            vector.tensor_scalar(gains[:, :], graw[:, :], 1.0, None,
                                 AO.add).then_inc(env.s_env, 1)
            vector.wait_ge(s_x, 16)
            ln_vector(ln, vector, env, env.actbuf, 0)

        def a_tensor(tensor):
            ln_tensor(ln, tensor, env, wait=(s_x, 16))

        def a_scalar(scalar):
            scalar.wait_ge(s_x, 16)
            ln_scalar(ln, scalar, env, CP_EPS5)

        run_block(nc, sync=a_sync, vector=a_vector, tensor=a_tensor,
                  scalar=a_scalar)
        ln_st.close()

        emit_l1qkv_body(nc, st_, env, ln, pj, q_sb, gains,
                        cosT, sinT, cosqT, sinqT, w_d, q_o, kst_o, vst_o)
    return nc


def emit_l1qkv_body(nc, st_, env, ln, pj, q_sb, gains, cosT, sinT, cosqT,
                    sinqT, w_d, q_o, kst_o, vst_o, env_ready=115, join=True):
    with ExitStack() as ps_:
        E = ps_.enter_context
        sqb = E(nc.sbuf_tensor("l1_sqb", [128, TL], BF16))
        nrow = E(nc.sbuf_tensor("l1_nrow", [1, 2 * TL], F32))
        rb = E(nc.sbuf_tensor("l1_rb", [128, TL], F32))
        scrA = E(nc.sbuf_tensor("l1_sA", [128, TL], F32))
        scrB = E(nc.sbuf_tensor("l1_sB", [128, TL], F32))
        rot = E(nc.sbuf_tensor("l1_rot", [128, TL], F32))
        kvb = E(nc.sbuf_tensor("l1_kvb", [128, 2 * TL], BF16))
        knorm = E(nc.sbuf_tensor("l1_knorm", [1, 2 * TL], BF16))
        vt_sb = E(nc.sbuf_tensor("l1_vt", [128, 2 * TL], BF16))
        s_sqq = env.sem_st.enter_context(nc.semaphore("l1_sqq"))
        s_nt = env.sem_st.enter_context(nc.semaphore("l1_nt"))
        s_nr = env.sem_st.enter_context(nc.semaphore("l1_nr"))
        s_bt = env.sem_st.enter_context(nc.semaphore("l1_bt"))
        s_kq = env.sem_st.enter_context(nc.semaphore("l1_kq"))
        s_knt = env.sem_st.enter_context(nc.semaphore("l1_knt"))
        s_knv = env.sem_st.enter_context(nc.semaphore("l1_knv"))
        s_trp = env.sem_st.enter_context(nc.semaphore("l1_trp"))
        s_trv = env.sem_st.enter_context(nc.semaphore("l1_trv"))
        s_kst = env.sem_st.enter_context(nc.semaphore("l1_kst"))
        s_vst = env.sem_st.enter_context(nc.semaphore("l1_vst"))
        s_st = env.sem_st.enter_context(nc.semaphore("l1_st"))
        proj_psum(nc, pj, ps_)
        ps_n = E(nc.psum_tensor("l1_pn", [1, TL], F32))
        ps_tr = [E(nc.psum_tensor("l1_ptr%d" % p, [128, 128], BF16))
                 for p in range(2)]
        ps_b = pj.ps[1]

        def rope_rms(vector, j, gaincol, ct, st, ps, out, out_off):
            vector.tensor_copy(rb[:, :], ps_b[:, :])
            vector.tensor_tensor(scrA[:, :], ps[:, :], rb[:, :], AO.mult)
            vector.tensor_scalar(scrA[:, :], scrA[:, :], gaincol, None,
                                 AO.mult)
            vector.tensor_scalar(rot[0:64, :], scrA[64:128, :], -1.0, None,
                                 AO.mult)
            vector.tensor_copy(rot[64:128, :], scrA[0:64, :])
            vector.tensor_tensor(scrB[:, :], scrA[:, :], ct[:, :], AO.mult)
            vector.tensor_tensor(rot[:, :], rot[:, :], st[:, :], AO.mult)
            return vector.tensor_tensor(out[:, out_off:out_off + TL],
                                        scrB[:, :], rot[:, :], AO.add)

        def b_gpsimd(gpsimd):
            proj_loads(pj, gpsimd, w_d)

        def b_tensor(tensor):
            tensor.wait_ge(ln.s_dv, ln.dv_done)
            for j in range(24):
                tensor.wait_ge(pj.s_wl, 16 * (j + 1))
                if j >= 1:
                    tensor.wait_ge(pj.s_ev, j)
                w = pj.wbuf[:, (j % 2) * CH * 128:((j % 2) + 1) * CH * 128]
                for h2 in range(2):
                    for i in range(CH):
                        ins = tensor.matmul(
                            pj.ps[0][:, h2 * TH:(h2 + 1) * TH],
                            w[:, i * 128:(i + 1) * 128],
                            env.actbuf[:, i * TL + h2 * TH:
                                       i * TL + h2 * TH + TH],
                            start=(i == 0), stop=(i == CH - 1))
                ins.then_inc(pj.s_mm, 1)
                if j < 20:
                    tensor.wait_ge(s_sqq, j + 1)
                    if 17 <= j <= 19:
                        tensor.wait_ge(s_knv, j - 16)
                    for h2 in range(2):
                        ins = tensor.matmul(ps_n[0:1, h2 * TH:(h2 + 1) * TH],
                                            env.ones_b[:, 0:1],
                                            sqb[:, h2 * TH:(h2 + 1) * TH],
                                            start=True, stop=True)
                    ins.then_inc(s_nt, 1)
                    tensor.wait_ge(s_nr, j + 1)
                    for h2 in range(2):
                        ins = tensor.matmul(ps_b[:, h2 * TH:(h2 + 1) * TH],
                                            one_row(env),
                                            nrow[0:1, h2 * TH:(h2 + 1) * TH],
                                            start=True, stop=True)
                    ins.then_inc(s_bt, 1)
                    if 16 <= j:
                        kvh = j - 16
                        tensor.wait_ge(s_kq, kvh + 1)
                        for h2 in range(2):
                            ins = tensor.matmul(
                                ps_n[0:1, h2 * TH:(h2 + 1) * TH],
                                env.ones_b[:, 0:1],
                                sqb[:, h2 * TH:(h2 + 1) * TH],
                                start=True, stop=True)
                        ins.then_inc(s_knt, 1)
                else:
                    kvh = j - 20
                    tensor.wait_ge(pj.s_ev, j + 1)
                    for qt in range(8):
                        m = kvh * 8 + qt
                        if m >= 2:
                            tensor.wait_ge(s_trv, m - 1)
                        tensor.transpose(
                            ps_tr[m % 2][:, :],
                            kvb[:, (j % 2) * TL + qt * 128:
                                (j % 2) * TL + (qt + 1) * 128],
                            env.idn[:, :]).then_inc(s_trp, 1)

        def b_scalar(scalar):
            scalar.wait_ge(env.s_cp, 1)
            for j in range(24):
                scalar.wait_ge(pj.s_mm, j + 1)
                if j < 20:
                    if j >= 1:
                        scalar.wait_ge(s_nt, j)
                    if 17 <= j <= 19:
                        scalar.wait_ge(s_knt, j - 16)
                    scalar.activation(sqb[:, :], pj.ps[0][:, :],
                                      AF.Square).then_inc(s_sqq, 1)
                    scalar.wait_ge(s_nt, j + 1)
                    scalar.activation(nrow[0:1, TL:2 * TL], ps_n[0:1, :],
                                      AF.Ln, scale=1.0 / 128,
                                      bias=env.cpool[0:1, CP_EPS6:CP_EPS6 + 1])
                    scalar.activation(nrow[0:1, 0:TL], nrow[0:1, TL:2 * TL],
                                      AF.Exp, scale=-0.5).then_inc(s_nr, 1)
                if 16 <= j < 20:
                    kvh = j - 16
                    scalar.wait_ge(pj.s_ev, j + 1)
                    if kvh >= 1:
                        scalar.wait_ge(s_knt, kvh)
                    scalar.activation(sqb[:, :],
                                      kvb[:, (j % 2) * TL:(j % 2 + 1) * TL],
                                      AF.Square).then_inc(s_kq, 1)
                    scalar.dma_start(
                        out=kst_o[kvh * 128:(kvh + 1) * 128, :],
                        in_=kvb[:, (j % 2) * TL:(j % 2 + 1) * TL]
                    ).then_inc(s_kst, 16)
                    scalar.wait_ge(s_knv, kvh + 1)
                    scalar.dma_start(
                        out=kst_o[512 + kvh:513 + kvh, :],
                        in_=knorm[0:1, (kvh % 2) * TL:(kvh % 2 + 1) * TL]
                    ).then_inc(s_st, 16)
                elif j >= 20:
                    kvh = j - 20
                    scalar.wait_ge(s_trv, 8 * (kvh + 1))
                    scalar.dma_start(
                        out=vst_o[kvh * TL:(kvh + 1) * TL, :]
                        .rearrange("(q p) d -> p q d", p=128),
                        in_=vt_sb[:, (kvh % 2) * TL:(kvh % 2 + 1) * TL]
                        .rearrange("p (q d) -> p q d", q=8)).then_inc(s_vst, 16)
            if q_o is not None:
                scalar.wait_ge(pj.s_ev, 16)
                scalar.dma_start(out=q_o[:, :]
                                 .rearrange("(c p) t -> p c t", p=128),
                                 in_=q_sb[:, :]
                                 .rearrange("p (c t) -> p c t", c=CH)
                                 ).then_inc(s_st, 16)
            if join:
                scalar.wait_ge(s_st, 16 * (5 if q_o is not None else 4))
                scalar.wait_ge(s_kst, 16 * 4)
                scalar.wait_ge(s_vst, 16 * 4)

        def b_vector(vector):
            vector.wait_ge(env.s_env, env_ready)
            for j in range(24):
                if j < 16:
                    vector.wait_ge(s_bt, j + 1)
                    rope_rms(vector, j, gains[:, 0:1], cosqT, sinqT,
                             pj.ps[0], q_sb, j * TL).then_inc(pj.s_ev, 1)
                elif j < 20:
                    kvh = j - 16
                    vector.wait_ge(s_bt, j + 1)
                    if kvh >= 2:
                        vector.wait_ge(s_kst, 16 * (kvh - 1))
                    rope_rms(vector, j, gains[:, 1:2], cosT, sinT,
                             pj.ps[0], kvb, (j % 2) * TL).then_inc(pj.s_ev, 1)
                    vector.wait_ge(s_knt, kvh + 1)
                    if kvh >= 2:
                        vector.wait_ge(s_st, 16 * (kvh - 1))
                    vector.tensor_copy(knorm[0:1, (kvh % 2) * TL:
                                             (kvh % 2 + 1) * TL],
                                       ps_n[0:1, :]).then_inc(s_knv, 1)
                else:
                    kvh = j - 20
                    vector.wait_ge(pj.s_mm, j + 1)
                    if kvh < 2:
                        vector.wait_ge(s_kst, 16 * (kvh + 3))
                    else:
                        vector.wait_ge(s_trp, 8 * (kvh - 1))
                    vector.tensor_copy(kvb[:, (j % 2) * TL:(j % 2 + 1) * TL],
                                       pj.ps[0][:, :]).then_inc(pj.s_ev, 1)
                    if kvh >= 2:
                        vector.wait_ge(s_vst, 16 * (kvh - 1))
                    for qt in range(8):
                        m = kvh * 8 + qt
                        vector.wait_ge(s_trp, m + 1)
                        vector.tensor_copy(
                            vt_sb[:, (kvh % 2) * TL + qt * 128:
                                  (kvh % 2) * TL + (qt + 1) * 128],
                            ps_tr[m % 2][:, :]).then_inc(s_trv, 1)

        run_block(nc, gpsimd=b_gpsimd, tensor=b_tensor, scalar=b_scalar,
                  vector=b_vector)
        return NS2(s_kst=s_kst, s_vst=s_vst, s_st=s_st)


def build_l1attn():
    nc = bass.Bass("TRN2", target_bir_lowering=False,
                   detect_race_conditions=False)
    cfg = AttnCfg(l1=True)
    q_d = nc.dram_tensor("q", [D, TL], BF16, kind="ExternalInput")
    kg_d = nc.dram_tensor("kg", [NC * cfg.kst_rows, TL], BF16,
                          kind="ExternalInput")
    vg_d = nc.dram_tensor("vg", [NC * cfg.vst_rows, 128], BF16,
                          kind="ExternalInput")
    t1_d = nc.dram_tensor("c_t1", [128, 896], F32, kind="ExternalInput")
    t2_d = nc.dram_tensor("c_t2", [128, 896], F32, kind="ExternalInput")
    ctx_o = nc.dram_tensor("ctx", [D, TL], BF16, kind="ExternalOutput")

    with ExitStack() as st_:
        E = st_.enter_context
        env = make_env(nc, st_)
        q_sb = E(nc.sbuf_tensor("q_sb", [128, CH * TL], BF16))
        t1 = E(nc.sbuf_tensor("t1", [128, 896], F32))
        t2 = E(nc.sbuf_tensor("t2", [128, 896], F32))
        s_q = E(nc.semaphore("s_q"))
        s_done = E(nc.semaphore("s_done"))
        s_att_dve = E(nc.semaphore("s_att_dve"))
        s_out = E(nc.semaphore("s_out"))

        def a_sync(sync):
            sync.dma_start(out=env.idn[:, :], in_=env.idn_d[:, :]
                           ).then_inc(env.s_env, 16)
            sync.dma_start(out=q_sb[:, :].rearrange("p (c t) -> p c t", c=CH),
                           in_=q_d[:, :].rearrange("(c p) t -> p c t", p=128)
                           ).then_inc(s_q, 16)
            sync.dma_start(out=t1[:, :], in_=t1_d[:, :]).then_inc(s_q, 16)
            sync.dma_start(out=t2[:, :], in_=t2_d[:, :]).then_inc(s_q, 16)

        def a_vector(vector):
            env_consts_vector(env, vector)
            vector.memset(env.ones_f[:, :], 1.0).then_inc(env.s_env, 1)
            vector.memset(env.ones_b[:, :], 1.0).then_inc(env.s_env, 1)
            vector.wait_ge(env.s_env, 18)
            vector.wait_ge(s_q, 48)
            vector.sem_inc(env.s_env, 1)

        run_block(nc, sync=a_sync, vector=a_vector)

        def gate(eng):
            eng.wait_ge(env.s_env, 19)

        with nc.Block() as blk:
            blk.tensor(gate)
            blk.scalar(gate)
            blk.gpsimd(gate)
            blk.sync(gate)

        emit_attn(nc, st_, env, cfg, q_sb, env.actbuf, kg_d, vg_d, t1, t2,
                  s_done, s_dve_out=s_att_dve)

        def z_sync(sync):
            sync.wait_ge(s_done, 5)
            sync.wait_ge(s_att_dve, 96)
            sync.dma_start(out=ctx_o[:, :].rearrange("(c p) t -> p c t", p=128),
                           in_=env.actbuf[:, :].rearrange("p (c t) -> p c t",
                                                          c=CH)
                           ).then_inc(s_out, 16)
            sync.wait_ge(s_out, 16)

        run_block(nc, sync=z_sync)
    return nc



def build_l1ffn():
    nc = bass.Bass("TRN2", target_bir_lowering=False,
                   detect_race_conditions=False)
    x_d = nc.dram_tensor("x", [D, TL], F32, kind="ExternalInput")
    xb_d = nc.dram_tensor("xb", [D, TL], BF16, kind="ExternalInput")
    wg_d = nc.dram_tensor("wg", [64, CH, 128, 128], F32, kind="ExternalInput")
    wu_d = nc.dram_tensor("wu", [64, CH, 128, 128], F32, kind="ExternalInput")
    wd_d = nc.dram_tensor("wd", [4, CH, CH, 128, 128], F32,
                          kind="ExternalInput")
    xo_d = nc.dram_tensor("xo", [D, TL], F32, kind="ExternalOutput")

    with ExitStack() as st_:
        E = st_.enter_context
        env = make_env(nc, st_)
        s_x = E(nc.semaphore("s_x"))
        s_st = E(nc.semaphore("s_st"))
        s_ev = E(nc.semaphore("s_ev"))

        def a_sync(sync):
            sync.dma_start(out=env.res[:, :].rearrange("p (c t) -> p c t", c=CH),
                           in_=x_d[:, :].rearrange("(c p) t -> p c t", p=128)
                           ).then_inc(s_x, 16)
            sync.dma_start(out=env.actbuf[:, :].rearrange("p (c t) -> p c t",
                                                          c=CH),
                           in_=xb_d[:, :].rearrange("(c p) t -> p c t", p=128)
                           ).then_inc(s_x, 16)

        def a_vector(vector):
            env_consts_vector(env, vector)
            vector.memset(env.ones_f[:, :], 1.0)
            vector.memset(env.ones_b[:, :], 1.0).then_inc(env.s_env, 1)

        run_block(nc, sync=a_sync, vector=a_vector)

        emit_l1ffn(nc, st_, env, wg_d, wu_d, wd_d, s_x, s_ev)

        def z_sync(sync):
            sync.wait_ge(s_ev, 64)
            sync.dma_start(out=xo_d[:, :].rearrange("(c p) t -> p c t", p=128),
                           in_=env.res[:, :].rearrange("p (c t) -> p c t", c=CH)
                           ).then_inc(s_st, 16)
            sync.wait_ge(s_st, 16)

        run_block(nc, sync=z_sync)
    return nc


def emit_l1ffn(nc, st_, env, wg_d, wu_d, wd_d, s_x, s_ev, x_wait=32,
               gp_wait=None):
    with ExitStack() as ps_:
        E = ps_.enter_context
        h2 = E(nc.sbuf_tensor("f1_h", [128, CH * TL], BF16))
        gwb = E(nc.sbuf_tensor("f1_gw", [128, 2 * 2048], BF16))
        uwb = E(nc.sbuf_tensor("f1_uw", [128, 2 * 2048], BF16))
        dwb = E(nc.sbuf_tensor("f1_dw", [128, 2 * 2048], BF16))
        sgb = E(nc.sbuf_tensor("f1_sg", [128, TL], BF16))
        s_wl = env.sem_st.enter_context(nc.semaphore("f1_wl"))
        s_dl = env.sem_st.enter_context(nc.semaphore("f1_dl"))
        s_mm = env.sem_st.enter_context(nc.semaphore("f1_mm"))
        s_sg = env.sem_st.enter_context(nc.semaphore("f1_sg"))
        s_hv = env.sem_st.enter_context(nc.semaphore("f1_hv"))
        s_dm = env.sem_st.enter_context(nc.semaphore("f1_dm"))
        ps_g = E(nc.psum_tensor("f1_pg", [128, TL], F32))
        ps_u = E(nc.psum_tensor("f1_pu", [128, TL], F32))
        ps_d = [E(nc.psum_tensor("f1_pd%d" % p, [128, TL], F32))
                for p in range(2)]

        def f_gpsimd(gpsimd):
            if gp_wait is not None:
                gpsimd.wait_ge(*gp_wait)
            nw = 0
            nd = 0
            for Q in range(4):
                for hc in range(CH):
                    iw = nw
                    nw += 1
                    if iw >= 2:
                        gpsimd.wait_ge(s_mm, 2 * iw - 2)
                    par = iw % 2
                    gpsimd.dma_start(
                        out=gwb[:, par * 2048:(par + 1) * 2048]
                        .rearrange("p (i f) -> p i f", i=CH),
                        in_=wg_d[Q * CH + hc, :, :, :]
                        .rearrange("i p f -> p i f")).then_inc(s_wl, 16)
                    gpsimd.dma_start(
                        out=uwb[:, par * 2048:(par + 1) * 2048]
                        .rearrange("p (i f) -> p i f", i=CH),
                        in_=wu_d[Q * CH + hc, :, :, :]
                        .rearrange("i p f -> p i f")).then_inc(s_wl, 16)
                for oc in range(CH):
                    idn_ = nd
                    nd += 1
                    if idn_ >= 2:
                        gpsimd.wait_ge(s_dm, idn_ - 1)
                    gpsimd.dma_start(
                        out=dwb[:, (idn_ % 2) * 2048:(idn_ % 2 + 1) * 2048]
                        .rearrange("p (i f) -> p i f", i=CH),
                        in_=wd_d[Q, oc, :, :, :]
                        .rearrange("i p f -> p i f")).then_inc(s_dl, 16)

        def f_tensor(tensor):
            tensor.wait_ge(s_x, x_wait)
            nw = 0
            nd = 0
            for Q in range(4):
                for hc in range(CH):
                    iw = nw
                    nw += 1
                    tensor.wait_ge(s_wl, 32 * (iw + 1))
                    if iw >= 1:
                        tensor.wait_ge(s_sg, iw)
                    par = iw % 2
                    for h2_ in range(2):
                        for i in range(CH):
                            ins = tensor.matmul(
                                ps_g[:, h2_ * TH:(h2_ + 1) * TH],
                                gwb[:, par * 2048 + i * 128:
                                    par * 2048 + (i + 1) * 128],
                                env.actbuf[:, i * TL + h2_ * TH:
                                           i * TL + h2_ * TH + TH],
                                start=(i == 0), stop=(i == CH - 1))
                    ins.then_inc(s_mm, 1)
                    if iw >= 1:
                        tensor.wait_ge(s_hv, iw)
                    for h2_ in range(2):
                        for i in range(CH):
                            ins = tensor.matmul(
                                ps_u[:, h2_ * TH:(h2_ + 1) * TH],
                                uwb[:, par * 2048 + i * 128:
                                    par * 2048 + (i + 1) * 128],
                                env.actbuf[:, i * TL + h2_ * TH:
                                           i * TL + h2_ * TH + TH],
                                start=(i == 0), stop=(i == CH - 1))
                    ins.then_inc(s_mm, 1)
                for oc in range(CH):
                    idn_ = nd
                    nd += 1
                    tensor.wait_ge(s_dl, 16 * (idn_ + 1))
                    tensor.wait_ge(s_hv, nw)
                    if idn_ >= 2:
                        tensor.wait_ge(s_ev, idn_ - 1)
                    for h2_ in range(2):
                        for i in range(CH):
                            ins = tensor.matmul(
                                ps_d[idn_ % 2][:, h2_ * TH:(h2_ + 1) * TH],
                                dwb[:, (idn_ % 2) * 2048 + i * 128:
                                    (idn_ % 2) * 2048 + (i + 1) * 128],
                                h2[:, i * TL + h2_ * TH: i * TL + h2_ * TH + TH],
                                start=(i == 0), stop=(i == CH - 1))
                    ins.then_inc(s_dm, 1)

        def f_scalar(scalar):
            nw = 0
            for Q in range(4):
                for hc in range(CH):
                    iw = nw
                    nw += 1
                    scalar.wait_ge(s_mm, 2 * iw + 1)
                    if iw >= 1:
                        scalar.wait_ge(s_hv, iw)
                    scalar.activation(sgb[:, :], ps_g[:, :],
                                      AF.Silu).then_inc(s_sg, 1)

        def f_vector(vector):
            nw = 0
            nd = 0
            for Q in range(4):
                for hc in range(CH):
                    iw = nw
                    nw += 1
                    vector.wait_ge(s_sg, iw + 1)
                    vector.wait_ge(s_mm, 2 * iw + 2)
                    vector.tensor_tensor(h2[:, hc * TL:(hc + 1) * TL],
                                         sgb[:, :], ps_u[:, :],
                                         AO.mult).then_inc(s_hv, 1)
                for oc in range(CH):
                    idn_ = nd
                    nd += 1
                    vector.wait_ge(s_dm, idn_ + 1)
                    vector.tensor_tensor(env.res[:, oc * TL:(oc + 1) * TL],
                                         env.res[:, oc * TL:(oc + 1) * TL],
                                         ps_d[idn_ % 2][:, :],
                                         AO.add).then_inc(s_ev, 1)

        run_block(nc, gpsimd=f_gpsimd, tensor=f_tensor, scalar=f_scalar,
                  vector=f_vector)



def build_fused():
    nc = bass.Bass("TRN2", target_bir_lowering=False,
                   detect_race_conditions=False)
    x_d = nc.dram_tensor("x", [D, TL], F32, kind="ExternalInput")
    wqkv_d = nc.dram_tensor("wqkv", [48, CH, 128, 128], F32, kind="ExternalInput")
    bq_d = nc.dram_tensor("bqkv", [128, 48], F32, kind="ExternalInput")
    wpr_d = nc.dram_tensor("wproj", [CH, CH, 128, 128], F32, kind="ExternalInput")
    bp_d = nc.dram_tensor("bproj", [128, CH], F32, kind="ExternalInput")
    gp_d = nc.dram_tensor("gu_pk", [1408, 256], I32, kind="ExternalInput")
    gs_d = nc.dram_tensor("gu_sc", [11, 128, 16], F16, kind="ExternalInput")
    dp_d = nc.dram_tensor("dn_pk", [256, 704], I32, kind="ExternalInput")
    ds_d = nc.dram_tensor("dn_sc", [2, 128, 44], F16, kind="ExternalInput")
    wq1_d = nc.dram_tensor("wqkv1", [24, CH, 128, 128], F32, kind="ExternalInput")
    qg_d = nc.dram_tensor("qn_w", [128, 1], F32, kind="ExternalInput")
    kg1w_d = nc.dram_tensor("kn_w", [128, 1], F32, kind="ExternalInput")
    wo_d = nc.dram_tensor("wo", [CH, CH, 128, 128], F32, kind="ExternalInput")
    wg_d = nc.dram_tensor("wg", [64, CH, 128, 128], F32, kind="ExternalInput")
    wu_d = nc.dram_tensor("wu", [64, CH, 128, 128], F32, kind="ExternalInput")
    wd_d = nc.dram_tensor("wd", [4, CH, CH, 128, 128], F32, kind="ExternalInput")
    t1_d = nc.dram_tensor("c_t1", [128, 896], F32, kind="ExternalInput")
    t2_d = nc.dram_tensor("c_t2", [128, 896], F32, kind="ExternalInput")
    cos_d = nc.dram_tensor("c_cos", [128, TL], F32, kind="ExternalInput")
    sin_d = nc.dram_tensor("c_sin", [128, TL], F32, kind="ExternalInput")
    cosq_d = nc.dram_tensor("c_cosq", [128, TL], F32, kind="ExternalInput")
    sinq_d = nc.dram_tensor("c_sinq", [128, TL], F32, kind="ExternalInput")
    xo_d = nc.dram_tensor("xo", [D, TL], F32, kind="ExternalOutput")
    cfg0 = AttnCfg(l1=False)
    cfg1 = AttnCfg(l1=True)
    kst0 = nc.dram_tensor("i_kst0", [cfg0.kst_rows, TL], BF16, kind="Internal")
    vst0 = nc.dram_tensor("i_vst0", [cfg0.vst_rows, 128], BF16, kind="Internal")
    kg0 = nc.dram_tensor("i_kg0", [NC * cfg0.kst_rows, TL], BF16, kind="Internal")
    vg0 = nc.dram_tensor("i_vg0", [NC * cfg0.vst_rows, 128], BF16,
                         kind="Internal")
    gust = nc.dram_tensor("i_gust", [176, 16384], BF16, kind="Internal")
    dnst = nc.dram_tensor("i_dnst", [88, 16384], BF16, kind="Internal")
    gug = nc.dram_tensor("i_gug", [NC * 176, 16384], BF16, kind="Internal")
    dng = nc.dram_tensor("i_dng", [NC * 88, 16384], BF16, kind="Internal")
    kst1 = nc.dram_tensor("i_kst1", [cfg1.kst_rows, TL], BF16, kind="Internal")
    vst1 = nc.dram_tensor("i_vst1", [cfg1.vst_rows, 128], BF16, kind="Internal")
    kg1 = nc.dram_tensor("i_kg1", [NC * cfg1.kst_rows, TL], BF16, kind="Internal")
    vg1 = nc.dram_tensor("i_vg1", [NC * cfg1.vst_rows, 128], BF16,
                         kind="Internal")
    fin_d = nc.dram_tensor("i_fin", [1, 16], F32, kind="Internal")
    fout_d = nc.dram_tensor("i_fout", [NC, 16], F32, kind="Internal")

    dbg = {}
    if FUSED_DEBUG:
        for nm in ("d_ctx0", "d_x1", "d_x2", "d_ctx1", "d_x3"):
            dbg[nm] = nc.dram_tensor(nm, [D, TL],
                                     BF16 if "ctx" in nm else F32,
                                     kind="ExternalOutput")
        dbg["d_bc0"] = nc.dram_tensor("d_bc0", [128, 32], F32,
                                      kind="ExternalOutput")
        dbg["d_rw0"] = nc.dram_tensor("d_rw0", [1, 256], F32,
                                      kind="ExternalOutput")
        dbg["d_kg0"] = nc.dram_tensor("d_kg0", [NC * 2064, TL], BF16,
                                      kind="ExternalOutput")
        dbg["d_kst0"] = nc.dram_tensor("d_kst0", [2064, TL], BF16,
                                       kind="ExternalOutput")
        dbg["d_gug"] = nc.dram_tensor("d_gug", [NC * 176, 16384], BF16,
                                      kind="ExternalOutput")
        dbg["d_gust"] = nc.dram_tensor("d_gust", [176, 16384], BF16,
                                       kind="ExternalOutput")
        dbg["d_xb2"] = nc.dram_tensor("d_xb2", [D, TL], BF16,
                                      kind="ExternalOutput")
        dbg["d_dng"] = nc.dram_tensor("d_dng", [NC * 88, 16384], BF16,
                                      kind="ExternalOutput")
        dbg["d_dnst"] = nc.dram_tensor("d_dnst", [88, 16384], BF16,
                                       kind="ExternalOutput")
        dbg["d_y0"] = nc.dram_tensor("d_y0", [D, TH], F32,
                                     kind="ExternalOutput")
    with ExitStack() as st_:
        E = st_.enter_context
        env = make_env(nc, st_)
        t1 = E(nc.sbuf_tensor("t1", [128, 896], F32))
        t2 = E(nc.sbuf_tensor("t2", [128, 896], F32))
        cosT = E(nc.sbuf_tensor("cosT", [128, TL], F32))
        sinT = E(nc.sbuf_tensor("sinT", [128, TL], F32))
        cosqT = E(nc.sbuf_tensor("cosqT", [128, TL], F32))
        sinqT = E(nc.sbuf_tensor("sinqT", [128, TL], F32))
        bq = E(nc.sbuf_tensor("bq", [128, 48], F32))
        bp = E(nc.sbuf_tensor("bp", [128, CH], F32))
        gains = E(nc.sbuf_tensor("gains", [128, 2], F32))
        graw = E(nc.sbuf_tensor("graw", [128, 2], F32))
        s_x = E(nc.semaphore("s_x"))
        s_coll = E(nc.semaphore("s_coll"))
        s_dqfin = E(nc.semaphore("s_dqfin"))
        s_ld = E(nc.semaphore("s_ld"))
        s_sc = E(nc.semaphore("s_sc"))
        s_done0 = E(nc.semaphore("s_done0"))
        s_adve0 = E(nc.semaphore("s_adve0"))
        s_done1 = E(nc.semaphore("s_done1"))
        s_adve1 = E(nc.semaphore("s_adve1"))
        s_fv0 = E(nc.semaphore("s_fv0"))
        s_ev1 = E(nc.semaphore("s_ev1"))
        s_out = E(nc.semaphore("s_out"))

        def p_sync(sync):
            for dst, src_ in ((env.idn, env.idn_d), (t1, t1_d), (t2, t2_d),
                              (cosT, cos_d), (sinT, sin_d), (cosqT, cosq_d),
                              (sinqT, sinq_d), (bq, bq_d), (bp, bp_d)):
                sync.dma_start(out=dst[:, :], in_=src_[:, :]
                               ).then_inc(env.s_env, 16)
            sync.dma_start(out=graw[:, 0:1], in_=qg_d[:, :]
                           ).then_inc(env.s_env, 16)
            sync.dma_start(out=graw[:, 1:2], in_=kg1w_d[:, :]
                           ).then_inc(env.s_env, 16)
            sync.dma_start(out=env.res[:, :].rearrange("p (c t) -> p c t", c=CH),
                           in_=x_d[:, :].rearrange("(c p) t -> p c t", p=128)
                           ).then_inc(s_x, 16)

        def p_vector(vector):
            env_consts_vector(env, vector)
            vector.memset(env.ones_f[:, :], 1.0).then_inc(env.s_env, 1)
            vector.memset(env.ones_b[:, :], 1.0).then_inc(env.s_env, 1)
            vector.wait_ge(env.s_env, 178)
            vector.tensor_scalar(gains[:, :], graw[:, :], 1.0, None,
                                 AO.add).then_inc(env.s_env, 1)

        run_block(nc, sync=p_sync, vector=p_vector)
        ENV_RDY = 179

        with ExitStack() as stB:
            q_sb = stB.enter_context(
                nc.sbuf_tensor("q_sb", [128, CH * TL], BF16))
            pjQ = proj_make(nc, st_, stB, "qkv", 48)
            with ExitStack() as ln_st:
                ln1 = ln_make(nc, st_, ln_st, "ln1")

                def b1_v(vector):
                    vector.wait_ge(s_x, 16)
                    ln_vector(ln1, vector, env, env.actbuf, 0)

                def b1_t(tensor):
                    ln_tensor(ln1, tensor, env, wait=(s_x, 16))

                def b1_s(scalar):
                    scalar.wait_ge(s_x, 16)
                    ln_scalar(ln1, scalar, env, CP_EPS5)

                run_block(nc, vector=b1_v, tensor=b1_t, scalar=b1_s)
            st0 = emit_l0qkv_body(nc, stB, env, ln1, pjQ, q_sb, bq, wqkv_d,
                                  None, kst0, vst0, env_ready=ENV_RDY,
                                  join=False)

            def cB_g(gpsimd):
                gpsimd.wait_ge(st0.s_kst, 16 * 16)
                gpsimd.wait_ge(st0.s_vst, 16 * 16)
                gpsimd.wait_ge(st0.s_st, 16 * 16)
                gpsimd.collective_compute(
                    "AllGather", AO.bypass, replica_groups=[list(range(NC))],
                    ins=[kst0.ap().opt()], outs=[kg0.ap().opt()]
                ).then_inc(s_coll, 1)
                gpsimd.collective_compute(
                    "AllGather", AO.bypass, replica_groups=[list(range(NC))],
                    ins=[vst0.ap().opt()], outs=[vg0.ap().opt()]
                ).then_inc(s_coll, 1)
                gpsimd.collective_compute(
                    "AllGather", AO.bypass, replica_groups=[list(range(NC))],
                    ins=[fin_d.ap().opt()], outs=[fout_d.ap().opt()]
                ).then_inc(s_coll, 1)

            run_block(nc, gpsimd=cB_g)

            emit_dequant(nc, st_, env, gp_d, gs_d, dp_d, ds_d, gust, dnst,
                         s_ld, s_sc, s_dqfin,
                         sync_wait=[(st0.s_kst, 256), (st0.s_vst, 256),
                                    (st0.s_st, 256)],
                         gate=[(s_coll, 3)])

            def cD_g(gpsimd):
                gpsimd.wait_ge(s_dqfin, 1)
                gpsimd.collective_compute(
                    "AllGather", AO.bypass, replica_groups=[list(range(NC))],
                    ins=[gust.ap().opt()], outs=[gug.ap().opt()]
                ).then_inc(s_coll, 1)
                gpsimd.collective_compute(
                    "AllGather", AO.bypass, replica_groups=[list(range(NC))],
                    ins=[dnst.ap().opt()], outs=[dng.ap().opt()]
                ).then_inc(s_coll, 1)
                gpsimd.collective_compute(
                    "AllGather", AO.bypass, replica_groups=[list(range(NC))],
                    ins=[fin_d.ap().opt()], outs=[fout_d.ap().opt()]
                ).then_inc(s_coll, 1)

            run_block(nc, gpsimd=cD_g)

            if FUSED_DEBUG:
                def dbg_kg(gpsimd):
                    gpsimd.wait_ge(s_coll, 6)
                    gpsimd.wait_ge(s_dqfin, 1)
                    gpsimd.dma_start(out=dbg["d_kg0"][:, :], in_=kg0[:, :]
                                     ).then_inc(s_out, 16)
                    gpsimd.dma_start(out=dbg["d_kst0"][:, :], in_=kst0[:, :]
                                     ).then_inc(s_out, 16)
                    gpsimd.dma_start(out=dbg["d_gug"][:, :], in_=gug[:, :]
                                     ).then_inc(s_out, 16)
                    gpsimd.dma_start(out=dbg["d_gust"][:, :], in_=gust[:, :]
                                     ).then_inc(s_out, 16)
                    gpsimd.dma_start(out=dbg["d_dng"][:, :], in_=dng[:, :]
                                     ).then_inc(s_out, 16)
                    gpsimd.dma_start(out=dbg["d_dnst"][:, :], in_=dnst[:, :]
                                     ).then_inc(s_out, 16)
                run_block(nc, gpsimd=dbg_kg)

            emit_attn(nc, st_, env, cfg0, q_sb, env.actbuf, kg0, vg0, t1, None,
                      s_done0, s_dve_out=s_adve0,
                      sync_wait=[(s_coll, 3), (s_dqfin, 1)],
                      scalar_wait=(pjQ.s_ev, 16), tag="a0",
                      gate=[(s_coll, 6), (s_dqfin, 1), (pjQ.s_ev, 48)],
                      dbg=((dbg["d_bc0"], dbg["d_rw0"])
                           if FUSED_DEBUG else None))

        if FUSED_DEBUG:
            def dbg_ctx0(sync):
                sync.wait_ge(s_done0, 5)
                sync.wait_ge(s_adve0, 96)
                sync.dma_start(out=dbg["d_ctx0"][:, :]
                               .rearrange("(c p) t -> p c t", p=128),
                               in_=env.actbuf[:, :]
                               .rearrange("p (c t) -> p c t", c=CH)
                               ).then_inc(s_out, 16)
            run_block(nc, sync=dbg_ctx0)

        ln2x = [None]
        ln3x = [None]
        with ExitStack() as stE:
            pjP = proj_make(nc, st_, stE, "proj", CH)
            with ExitStack() as ps_:
                proj_psum(nc, pjP, ps_)

                def e_g(gpsimd):
                    proj_loads(pjP, gpsimd, wpr_d, first_wait=(s_done0, 5))

                def e_t(tensor):
                    tensor.wait_ge(s_adve0, 96)
                    for j in range(CH):
                        proj_chunk_tensor(pjP, tensor, j, env.actbuf, 0)

                def e_v(vector):
                    for j in range(CH):
                        vector.wait_ge(pjP.s_mm, j + 1)
                        vector.scalar_tensor_tensor(
                            env.res[:, j * TL:(j + 1) * TL],
                            pjP.ps[j % 2][:, :], bp[:, j:j + 1],
                            env.res[:, j * TL:(j + 1) * TL],
                            AO.add, AO.add).then_inc(pjP.s_ev, 1)

                run_block(nc, gpsimd=e_g, tensor=e_t, vector=e_v)
            with ExitStack() as ln_st:
                ln2 = ln_make(nc, st_, ln_st, "ln2", prev=ln1)

                def e2_v(vector):
                    ln_vector(ln2, vector, env, env.actbuf, 0)

                def e2_t(tensor):
                    ln_tensor(ln2, tensor, env, wait=(pjP.s_ev, CH))

                def e2_s(scalar):
                    scalar.wait_ge(pjP.s_ev, CH)
                    ln_scalar(ln2, scalar, env, CP_EPS5)

                run_block(nc, vector=e2_v, tensor=e2_t, scalar=e2_s)
                ln2x[0] = ln2

        if FUSED_DEBUG:
            def dbg_x1(sync):
                sync.wait_ge(ln2x[0].s_dv, ln2x[0].dv_done)
                sync.dma_start(out=dbg["d_x1"][:, :]
                               .rearrange("(c p) t -> p c t", p=128),
                               in_=env.res[:, :]
                               .rearrange("p (c t) -> p c t", c=CH)
                               ).then_inc(s_out, 16)
            run_block(nc, sync=dbg_x1)

        emit_l0ffn(nc, st_, env, gug, dng, ln2.s_dv, s_fv0,
                   x_wait=ln2.dv_done,
                   sync_wait=[(s_coll, 6), (ln2.s_dv, ln2.dv_done)],
                   tensor_waits=[(s_coll, 6)],
                   dbg_y=((dbg["d_y0"], s_out) if FUSED_DEBUG else None))

        if FUSED_DEBUG:
            def dbg_xb2(sync):
                sync.wait_ge(ln2x[0].s_dv, ln2x[0].dv_done)
                sync.dma_start(out=dbg["d_xb2"][:, :]
                               .rearrange("(c p) t -> p c t", p=128),
                               in_=env.actbuf[:, :]
                               .rearrange("p (c t) -> p c t", c=CH)
                               ).then_inc(s_out, 16)
            run_block(nc, sync=dbg_xb2)

        if FUSED_DEBUG:
            def dbg_x2(sync):
                sync.wait_ge(s_fv0, 2)
                sync.dma_start(out=dbg["d_x2"][:, :]
                               .rearrange("(c p) t -> p c t", p=128),
                               in_=env.res[:, :]
                               .rearrange("p (c t) -> p c t", c=CH)
                               ).then_inc(s_out, 16)
            run_block(nc, sync=dbg_x2)

        with ExitStack() as stG:
            q_sb1 = stG.enter_context(
                nc.sbuf_tensor("q_sb1", [128, CH * TL], BF16))
            pjQ1 = proj_make(nc, st_, stG, "qkv1", 24)
            with ExitStack() as ln_st:
                ln3 = ln_make(nc, st_, ln_st, "ln3", prev=ln2x[0])

                def g_v(vector):
                    ln_vector(ln3, vector, env, env.actbuf, 0)

                def g_t(tensor):
                    ln_tensor(ln3, tensor, env, wait=(s_fv0, 2))

                def g_s(scalar):
                    scalar.wait_ge(s_fv0, 2)
                    ln_scalar(ln3, scalar, env, CP_EPS5)

                run_block(nc, vector=g_v, tensor=g_t, scalar=g_s)
                ln3x[0] = ln3
            st1 = emit_l1qkv_body(nc, stG, env, ln3, pjQ1, q_sb1, gains,
                                  cosT, sinT, cosqT, sinqT, wq1_d, None,
                                  kst1, vst1, env_ready=ENV_RDY, join=False)

            def h_g(gpsimd):
                gpsimd.wait_ge(st1.s_kst, 16 * 4)
                gpsimd.wait_ge(st1.s_vst, 16 * 4)
                gpsimd.wait_ge(st1.s_st, 16 * 4)
                gpsimd.collective_compute(
                    "AllGather", AO.bypass, replica_groups=[list(range(NC))],
                    ins=[kst1.ap().opt()], outs=[kg1.ap().opt()]
                ).then_inc(s_coll, 1)
                gpsimd.collective_compute(
                    "AllGather", AO.bypass, replica_groups=[list(range(NC))],
                    ins=[vst1.ap().opt()], outs=[vg1.ap().opt()]
                ).then_inc(s_coll, 1)
                gpsimd.collective_compute(
                    "AllGather", AO.bypass, replica_groups=[list(range(NC))],
                    ins=[fin_d.ap().opt()], outs=[fout_d.ap().opt()]
                ).then_inc(s_coll, 1)

            run_block(nc, gpsimd=h_g)

            emit_attn(nc, st_, env, cfg1, q_sb1, env.actbuf, kg1, vg1, t1, t2,
                      s_done1, s_dve_out=s_adve1, sync_wait=[(s_coll, 9)],
                      scalar_wait=(pjQ1.s_ev, 16), tag="a1")

        if FUSED_DEBUG:
            def dbg_ctx1(sync):
                sync.wait_ge(s_done1, 5)
                sync.wait_ge(s_adve1, 96)
                sync.dma_start(out=dbg["d_ctx1"][:, :]
                               .rearrange("(c p) t -> p c t", p=128),
                               in_=env.actbuf[:, :]
                               .rearrange("p (c t) -> p c t", c=CH)
                               ).then_inc(s_out, 16)
            run_block(nc, sync=dbg_ctx1)

        with ExitStack() as stI:
            pjO = proj_make(nc, st_, stI, "wo", CH)
            with ExitStack() as ps_:
                proj_psum(nc, pjO, ps_)

                def i_g(gpsimd):
                    proj_loads(pjO, gpsimd, wo_d, first_wait=(s_done1, 5))

                def i_t(tensor):
                    tensor.wait_ge(s_adve1, 96)
                    for j in range(CH):
                        proj_chunk_tensor(pjO, tensor, j, env.actbuf, 0)

                def i_v(vector):
                    for j in range(CH):
                        vector.wait_ge(pjO.s_mm, j + 1)
                        vector.scalar_tensor_tensor(
                            env.res[:, j * TL:(j + 1) * TL],
                            pjO.ps[j % 2][:, :], 0.0,
                            env.res[:, j * TL:(j + 1) * TL],
                            AO.add, AO.add).then_inc(pjO.s_ev, 1)

                run_block(nc, gpsimd=i_g, tensor=i_t, vector=i_v)
            with ExitStack() as ln_st:
                ln4 = ln_make(nc, st_, ln_st, "ln4", prev=ln3x[0])

                def i2_v(vector):
                    ln_vector(ln4, vector, env, env.actbuf, 0)

                def i2_t(tensor):
                    ln_tensor(ln4, tensor, env, wait=(pjO.s_ev, CH))

                def i2_s(scalar):
                    scalar.wait_ge(pjO.s_ev, CH)
                    ln_scalar(ln4, scalar, env, CP_EPS5)

                run_block(nc, vector=i2_v, tensor=i2_t, scalar=i2_s)

        if FUSED_DEBUG:
            def dbg_x3(sync):
                sync.wait_ge(pjO.s_ev, CH)
                sync.dma_start(out=dbg["d_x3"][:, :]
                               .rearrange("(c p) t -> p c t", p=128),
                               in_=env.res[:, :]
                               .rearrange("p (c t) -> p c t", c=CH)
                               ).then_inc(s_out, 16)
            run_block(nc, sync=dbg_x3)

        emit_l1ffn(nc, st_, env, wg_d, wu_d, wd_d, ln4.s_dv, s_ev1,
                   x_wait=ln4.dv_done, gp_wait=(ln4.s_dv, ln4.dv_done))

        n_dbg = 12 if FUSED_DEBUG else 0

        def z_sync(sync):
            sync.wait_ge(s_ev1, 64)
            sync.dma_start(out=xo_d[:, :].rearrange("(c p) t -> p c t", p=128),
                           in_=env.res[:, :].rearrange("p (c t) -> p c t", c=CH)
                           ).then_inc(s_out, 16)
            sync.wait_ge(s_out, 16 * (n_dbg + 1))

        run_block(nc, sync=z_sync)
    return nc


def kernel_fused(inp):
    cores = list(range(NC))
    idn = np.eye(128, dtype=BF)
    t1m, t2m = mask_tables()
    x0 = np.ascontiguousarray(np.asarray(inp["L0_attn_x"])[0])
    xs = shard_x(x0)
    wqkv_t = retile_io(np.asarray(inp["L0_attn_w_qkv"]))
    bq = bias_cols(np.asarray(inp["L0_attn_b_qkv"]))
    wproj_t = retile_io(np.asarray(inp["L0_attn_w_proj"]))
    bp = bias_cols(np.asarray(inp["L0_attn_b_proj"]))
    gup = np.asarray(inp["L0_ffn_gate_up_packed"])
    gus = np.asarray(inp["L0_ffn_gate_up_scales"])
    dnp = np.asarray(inp["L0_ffn_down_packed"])
    dns = np.asarray(inp["L0_ffn_down_scales"])
    w_all1 = np.concatenate(
        [retile_io(np.ascontiguousarray(np.asarray(inp["L1_attn_Wq"]).T)),
         retile_io(np.ascontiguousarray(np.asarray(inp["L1_attn_Wk"]).T)),
         retile_io(np.ascontiguousarray(np.asarray(inp["L1_attn_Wv"]).T))],
        axis=0)
    wo_t = retile_io(np.ascontiguousarray(np.asarray(inp["L1_attn_Wo"]).T))
    wg_t = retile_io(np.ascontiguousarray(np.asarray(inp["L1_ffn_w_gate"]).T))
    wu_t = retile_io(np.ascontiguousarray(np.asarray(inp["L1_ffn_w_up"]).T))
    wdt = retile_io(np.ascontiguousarray(np.asarray(inp["L1_ffn_w_down"]).T))
    wd_t = np.ascontiguousarray(
        wdt.reshape(CH, 4, CH, 128, 128).transpose(1, 0, 2, 3, 4))
    qnw = np.ascontiguousarray(np.asarray(inp["L1_attn_qn_w"]).reshape(128, 1))
    knw = np.ascontiguousarray(np.asarray(inp["L1_attn_kn_w"]).reshape(128, 1))
    ropes = [rope_tables(c) for c in cores]

    in_maps = []
    for c in cores:
        in_maps.append({
            "x": xs[c], "wqkv": wqkv_t, "bqkv": bq, "wproj": wproj_t,
            "bproj": bp,
            "gu_pk": np.ascontiguousarray(gup[1408 * c:1408 * (c + 1)]),
            "gu_sc": dequant_scales_host(gus, 1408 * c, 1408, 11),
            "dn_pk": np.ascontiguousarray(dnp[256 * c:256 * (c + 1)]),
            "dn_sc": dequant_scales_host(dns, 256 * c, 256, 2),
            "wqkv1": w_all1, "qn_w": qnw, "kn_w": knw, "wo": wo_t,
            "wg": wg_t, "wu": wu_t, "wd": wd_t,
            "c_t1": t1m, "c_t2": t2m, "c_cos": ropes[c][0],
            "c_sin": ropes[c][1], "c_cosq": ropes[c][2],
            "c_sinq": ropes[c][3], "c_idn": idn,
        })
    res = run_bass_kernel_spmd(build_fused(), in_maps, core_ids=cores)
    out = unshard_x([np.asarray(res.results[c]["xo"]) for c in cores])
    return out[None, :, :].astype(np.float32)



def kernel(**inputs):
    return kernel_fused(inputs)


def kernel_multi(L0_attn_x, L0_attn_w_qkv, L0_attn_b_qkv, L0_attn_w_proj,
                 L0_attn_b_proj, L0_ffn_gate_up_packed, L0_ffn_gate_up_scales,
                 L0_ffn_down_packed, L0_ffn_down_scales, L1_attn_Wq,
                 L1_attn_Wk, L1_attn_Wv, L1_attn_Wo, L1_attn_qn_w,
                 L1_attn_kn_w, L1_ffn_w_gate, L1_ffn_w_up, L1_ffn_w_down):
    cores = list(range(NC))
    idn = np.eye(128, dtype=BF)
    t1m, t2m = mask_tables()
    x0 = np.ascontiguousarray(np.asarray(L0_attn_x)[0])
    xs = shard_x(x0)

    wqkv_t = retile_io(np.asarray(L0_attn_w_qkv))
    bq = bias_cols(np.asarray(L0_attn_b_qkv))
    r1 = run_bass_kernel_spmd(
        build_l0qkv(),
        [{"x": xs[c], "wqkv": wqkv_t, "bqkv": bq, "c_idn": idn}
         for c in cores], core_ids=cores).results
    Kg = np.concatenate([np.asarray(r1[c]["kstage"]) for c in cores], axis=0)
    Vg = np.concatenate([np.asarray(r1[c]["vstage"]) for c in cores], axis=0)

    gup = np.asarray(L0_ffn_gate_up_packed)
    gus = np.asarray(L0_ffn_gate_up_scales)
    dnp = np.asarray(L0_ffn_down_packed)
    dns = np.asarray(L0_ffn_down_scales)
    r4 = run_bass_kernel_spmd(
        build_dequant(),
        [{"gu_pk": np.ascontiguousarray(gup[1408 * c:1408 * (c + 1)]),
          "gu_sc": dequant_scales_host(gus, 1408 * c, 1408, 11),
          "dn_pk": np.ascontiguousarray(dnp[256 * c:256 * (c + 1)]),
          "dn_sc": dequant_scales_host(dns, 256 * c, 256, 2),
          "c_idn": idn} for c in cores], core_ids=cores).results
    GUg = np.concatenate([np.asarray(r4[c]["gustage"]) for c in cores], axis=0)
    DNg = np.concatenate([np.asarray(r4[c]["dstage"]) for c in cores], axis=0)

    r2 = run_bass_kernel_spmd(
        build_l0attn(),
        [{"q": np.asarray(r1[c]["q"]), "kg": Kg, "vg": Vg, "c_t1": t1m,
          "c_idn": idn} for c in cores], core_ids=cores).results

    wproj_t = retile_io(np.asarray(L0_attn_w_proj))
    bp = bias_cols(np.asarray(L0_attn_b_proj))
    r3 = run_bass_kernel_spmd(
        build_proj_res_ln(True, "2"),
        [{"x": xs[c], "ctx": np.asarray(r2[c]["ctx"]), "wproj": wproj_t,
          "bproj": bp, "c_idn": idn} for c in cores], core_ids=cores).results

    r5 = run_bass_kernel_spmd(
        build_l0ffn(),
        [{"x": np.asarray(r3[c]["xo"]), "xb": np.asarray(r3[c]["xb"]),
          "gug": GUg, "dng": DNg, "c_idn": idn}
         for c in cores], core_ids=cores).results

    w_all1 = np.concatenate(
        [retile_io(np.ascontiguousarray(np.asarray(L1_attn_Wq).T)),
         retile_io(np.ascontiguousarray(np.asarray(L1_attn_Wk).T)),
         retile_io(np.ascontiguousarray(np.asarray(L1_attn_Wv).T))], axis=0)
    ropes = [rope_tables(c) for c in cores]
    r6 = run_bass_kernel_spmd(
        build_l1qkv(),
        [{"x": np.asarray(r5[c]["xo"]), "wqkv1": w_all1,
          "qn_w": np.ascontiguousarray(np.asarray(L1_attn_qn_w).reshape(128, 1)),
          "kn_w": np.ascontiguousarray(np.asarray(L1_attn_kn_w).reshape(128, 1)),
          "c_cos": ropes[c][0], "c_sin": ropes[c][1],
          "c_cosq": ropes[c][2], "c_sinq": ropes[c][3], "c_idn": idn}
         for c in cores], core_ids=cores).results
    Kg1 = np.concatenate([np.asarray(r6[c]["kstage1"]) for c in cores], axis=0)
    Vg1 = np.concatenate([np.asarray(r6[c]["vstage1"]) for c in cores], axis=0)

    r7 = run_bass_kernel_spmd(
        build_l1attn(),
        [{"q": np.asarray(r6[c]["q2"]), "kg": Kg1, "vg": Vg1, "c_t1": t1m,
          "c_t2": t2m, "c_idn": idn} for c in cores], core_ids=cores).results

    wo_t = retile_io(np.ascontiguousarray(np.asarray(L1_attn_Wo).T))
    r8 = run_bass_kernel_spmd(
        build_proj_res_ln(False, "4"),
        [{"x": np.asarray(r5[c]["xo"]), "ctx": np.asarray(r7[c]["ctx"]),
          "wproj": wo_t, "c_idn": idn} for c in cores], core_ids=cores).results

    wg_t = retile_io(np.ascontiguousarray(np.asarray(L1_ffn_w_gate).T))
    wu_t = retile_io(np.ascontiguousarray(np.asarray(L1_ffn_w_up).T))
    wdt = retile_io(np.ascontiguousarray(np.asarray(L1_ffn_w_down).T))
    wd_t = np.ascontiguousarray(
        wdt.reshape(CH, 4, CH, 128, 128).transpose(1, 0, 2, 3, 4))
    r9 = run_bass_kernel_spmd(
        build_l1ffn(),
        [{"x": np.asarray(r8[c]["xo"]), "xb": np.asarray(r8[c]["xb"]),
          "wg": wg_t, "wu": wu_t, "wd": wd_t, "c_idn": idn}
         for c in cores], core_ids=cores).results

    out = unshard_x([np.asarray(r9[c]["xo"]) for c in cores])
    return out[None, :, :].astype(np.float32)